# version 30
# baseline (speedup 1.0000x reference)
import numpy as np
import ml_dtypes
import concourse.bass as bass
import concourse.mybir as mybir
from concourse.bass_utils import run_bass_kernel_spmd
from concourse.bass_core_rust import IndirectOffsetOnAxis
from contextlib import ExitStack

F32 = mybir.dt.float32
BF16 = mybir.dt.bfloat16
I32 = mybir.dt.int32
AF = mybir.ActivationFunctionType
OP = mybir.AluOpType
AX = mybir.AxisListType

B, N, D, H = 4, 4096, 1024, 16
TOK = 2048
CH = 512
NCH = TOK // CH
NT = TOK // 128
KC = D // 128
NEXP, CAP = 8, 5120
SC = 512
NSC = CAP // SC
HDIM = 4096
HKC = HDIM // 128
EPS = 1e-5
IBIG = 1 << 24

DBG = False
DBG2 = False


class Rec:

    def __init__(self, name, self_serialize):
        self.name, self.self_ser = name, self_serialize
        self.ops, self.n, self.sem = [], 0, None
        self.aux = {}
        self.flag = None
        self.flag_n = 0

    def op(self, fn, deps=(), inc=1, aux=None):
        if aux is not None:
            s = self.aux[aux]
            def wrap(e, fn=fn, s=s):
                inst = fn(e)
                inst.then_inc(s[0], 16)
                return inst
            self.ops.append((tuple(d for d in deps if d is not None), wrap, 0))
            s[1] += 16
            return (s[0], s[1])
        self.ops.append((tuple(d for d in deps if d is not None), fn, inc))
        self.n += inc
        return (self, self.n)

    def wait(self, deps):
        self.ops.append((tuple(d for d in deps if d is not None), None, 0))

    def flush(self):
        deps = [(self, self.n)]
        for s in self.aux.values():
            deps.append((s[0], s[1]))
        self.flag_n += 1
        self.ops.append((tuple(deps),
                         lambda e, f=self.flag: e.sem_inc(f, 1), 0))
        return (self.flag, self.flag_n)

    def tok(self):
        return (self, self.n)

    def take(self):
        ops, self.ops = self.ops, []
        return ops


def _noinst(x):
    return x


_BCREG = {}


def _bc(e, val):
    key = (id(e), val)
    if key not in _BCREG or _BCREG[key][0] is not e:
        _BCREG[key] = (e, e.to_reg(val))
    return _BCREG[key][1]


def _replay(eng, rec, ops, base):
    cnt = base
    for deps, fn, inc in ops:
        for dsem, dval in deps:
            sem = dsem.sem if isinstance(dsem, Rec) else dsem
            eng.wait_ge(sem, dval)
        if fn is None:
            continue
        if rec.self_ser and cnt:
            eng.wait_ge(rec.sem, cnt)
        inst = fn(eng)
        if inc:
            inst.then_inc(rec.sem, inc)
        cnt += inc
    return cnt


def build_nc():
    nc = bass.Bass("TRN2", target_bir_lowering=False, debug=False)

    def din(name, shape, dt=F32):
        return nc.dram_tensor(name, shape, dt, kind="ExternalInput").ap()

    dxch = din("xch", [D, TOK + 1])
    dxtok = din("xtok", [TOK, D])
    dwT = {p: din(f"w{p}T", [D, D]) for p in "gkvr"}
    dwoT = din("woT", [D, D])
    dwon = din("won", [D, D])
    dbias = {p: din(f"b{p}", [128, KC]) for p in "gkvr"}
    dwob = din("wob", [1, D])
    dmixv = {p: din(f"m{p}", [128, KC]) for p in "gkvr"}
    dwdec = din("wdec", [128, KC])
    du = din("uu", [128, KC])
    dwg = din("wg", [D, NEXP])
    dcfc = din("cfc", [D, HDIM])
    dcpr = din("cpr", [HDIM, D])
    dlt128 = din("lt128", [128, 128])
    dlt16 = din("lt16", [16, 16])
    dones = din("ones128", [128, 128])
    did8 = din("id8", [8, 8])
    diota8 = din("iota8", [128, NEXP])
    diorev = din("iorev", [128, CH])
    dindA = din("indA", [D, H])
    dindB = din("indB", [H, D])
    dselc = din("selc", [8, 1])
    dselp = din("selp", [8, 2])
    dtokid = din("tokid", [128, NT])
    dwobc = din("wobc", [128, KC])
    dtabix = din("tabix", [128, 1], I32)
    dmask8 = din("mask8", [128, 8])
    dsel16 = din("sel16", [128, 16])

    dout = nc.dram_tensor("out", [TOK, D], F32, kind="ExternalOutput").ap()
    if DBG:
        dx1d = nc.dram_tensor("x1d", [TOK, D], F32, kind="ExternalOutput").ap()
    if DBG2:
        dkvd = nc.dram_tensor("kvd", [KC * 128, TOK], F32, kind="ExternalOutput").ap()
        daccd = nc.dram_tensor("accd", [KC * 128, TOK], F32, kind="ExternalOutput").ap()
        dxmk = nc.dram_tensor("xmk", [KC * 128, TOK], BF16, kind="ExternalOutput").ap()
        dxmv = nc.dram_tensor("xmv", [KC * 128, TOK], BF16, kind="ExternalOutput").ap()
        dkev = nc.dram_tensor("kev", [KC * 128, TOK], F32, kind="ExternalOutput").ap()

    kv_sp = nc.dram_tensor("kv_sp", [KC, 128, TOK], F32).ap()
    sg_sp = nc.dram_tensor("sg_sp", [KC, 128, TOK], BF16).ap()
    rb_sp = nc.dram_tensor("rb_sp", [KC, 128, TOK], BF16).ap()
    x1_sp = nc.dram_tensor("x1_sp", [TOK, D], F32).ap()
    carry_in = nc.dram_tensor("carry_in", [KC, 128], F32).ap()
    carry_ag = nc.dram_tensor("carry_ag", [8, KC * 128], F32).ap()
    cnt_in = nc.dram_tensor("cnt_in", [1, 16], F32).ap()
    cnt_ag = nc.dram_tensor("cnt_ag", [8, 16], F32).ap()
    tot_b1 = nc.dram_tensor("tot_b1", [16, 8], F32).ap()
    tot_b2 = nc.dram_tensor("tot_b2", [16, 8], F32).ap()
    toff_b1 = nc.dram_tensor("toff_b1", [16, 8], F32).ap()
    toff_b2 = nc.dram_tensor("toff_b2", [16, 8], F32).ap()
    tab_in = nc.dram_tensor("tab_in", [NEXP * CAP, 2], F32).ap()
    tab_ar = nc.dram_tensor("tab_ar", [NEXP * CAP, 2], F32).ap()
    xf_in = nc.dram_tensor("xf_in", [TOK, D], BF16).ap()
    xf_ag = nc.dram_tensor("xf_ag", [8 * TOK, D], BF16).ap()
    xf_subH = [nc.dram_tensor(f"xf_subH{q}", [8 * 2 * CH, D], BF16).ap()
               for q in range(2)]
    batch_d = nc.dram_tensor("batch_d", [CAP, D], BF16).ap()
    cfc_b = nc.dram_tensor("cfc_b", [D, HDIM], BF16).ap()
    cpr_b = nc.dram_tensor("cpr_b", [HDIM, D], BF16).ap()
    rs_in = nc.dram_tensor("rs_in", [8 * TOK, D], BF16).ap()
    rs_out = nc.dram_tensor("rs_out", [TOK, D], BF16).ap()

    V = Rec("vector", True)
    S = Rec("scalar", True)
    T = Rec("tensor", False)
    Q = Rec("sync", False)
    G = Rec("gpsimd", False)
    RECS = (V, S, T, Q, G)
    bases = {r.name: 0 for r in RECS}

    with ExitStack() as top:
        for r in RECS:
            r.sem = top.enter_context(nc.semaphore(f"sem_{r.name}"))
        cag = top.enter_context(nc.semaphore("cag"))
        ccnt = top.enter_context(nc.semaphore("ccnt"))
        ctab = top.enter_context(nc.semaphore("ctab"))
        cxf = top.enter_context(nc.semaphore("cxf"))
        cxfs = [cxf] + [top.enter_context(nc.semaphore(f"cxf{q}"))
                        for q in range(1, NCH)]
        crs0 = top.enter_context(nc.semaphore("crs0"))
        crs1 = top.enter_context(nc.semaphore("crs1"))
        Q.flag = top.enter_context(nc.semaphore("qflag"))
        G.flag = top.enter_context(nc.semaphore("gflag"))
        gcomp = top.enter_context(nc.semaphore("gcomp"))
        gcnt = [0]

        def pop(fn, deps=()):
            mydeps = tuple(d for d in deps if d is not None)
            prev = gcnt[0]
            gcnt[0] += 1
            def wrap(e, fn=fn, prev=prev):
                if prev:
                    e.wait_ge(gcomp, prev)
                inst = fn(e)
                inst.then_inc(gcomp, 1)
                return inst
            G.ops.append((mydeps, wrap, 0))
            return (gcomp, gcnt[0])
        for nm in ("qx", "qds0", "qds1", "qxt0", "qxt1", "qm32",
                   "qxf0", "qxf1",
                   "qbch0", "qbch1", "qst0", "qst1", "qf0", "qf1",
                   "qxp0", "qxp1", "qso0", "qso1"):
            Q.aux[nm] = [top.enter_context(nc.semaphore(nm)), 0]
        for nm in ("sbch0", "sbch1", "sxt0", "sxt1"):
            S.aux[nm] = [top.enter_context(nc.semaphore(nm)), 0]
        for nm in ("ggt0", "ggt1", "geo0", "geo1", "geo2", "geo3",
                   "geo4", "geo5", "geo6", "geo7", "gxfc", "gpc"):
            G.aux[nm] = [top.enter_context(nc.semaphore(nm)), 0]
        Q.aux["qm32b"] = [top.enter_context(nc.semaphore("qm32b")), 0]

        def sbuf(stk, name, shape, dt):
            return stk.enter_context(nc.sbuf_tensor(name, shape, dt))

        pb = [top.enter_context(nc.psum_tensor(f"pb{i}", [128, 512], F32))
              for i in range(8)]

        def dma(rec, out, in_, deps=(), aux=None):
            return rec.op(lambda e, o=out, i=in_: e.dma_start(out=o, in_=i),
                          deps=deps, inc=16, aux=aux)

        def coll(kind, op, ins, outs, sem, deps=()):
            def f(e, kind=kind, op=op, ins=ins, outs=outs, sem=sem):
                return e.collective_compute(
                    kind, op, replica_groups=[list(range(8))],
                    ins=[a.opt() for a in ins],
                    outs=[a.opt() for a in outs]).then_inc(sem, 1)
            G.ops.append((tuple(d for d in deps if d is not None), f, 0))
            return (sem, 1)

        def run_block(stk):
            blk = stk.enter_context(nc.Block())
            for r in RECS:
                ops = r.take()
                if not ops:
                    continue
                def body(eng, r=r, ops=ops, base=bases[r.name]):
                    _replay(eng, r, ops, base)
                getattr(blk, r.name)(body)
                bases[r.name] = r.n

        s_eps = sbuf(top, "s_eps", [128, 1], F32)
        s_mhalf = sbuf(top, "s_mhalf", [128, 1], F32)
        s_col = sbuf(top, "s_col", [128, 8], F32)
        s_tbix = sbuf(top, "s_tbix", [128, 1], I32)
        P12 = top.enter_context(ExitStack())
        P = P12
        s_bias = {p: sbuf(P, f"s_b{p}", [128, KC], F32) for p in "gkvr"}
        s_mix = {p: sbuf(P, f"s_m{p}", [128, KC], F32) for p in "gkvr"}
        s_om = {p: sbuf(P, f"s_om{p}", [128, KC], F32) for p in "gkvr"}
        s_dec = sbuf(P, "s_dec", [128, KC], F32)
        s_e = sbuf(P, "s_e", [128, KC], F32)
        s_u = sbuf(P, "s_u", [128, KC], F32)
        s_d512 = sbuf(P, "s_d512", [128, KC * 3], F32)
        s_lt128 = sbuf(P, "s_lt128", [128, 128], F32)
        s_lt16 = sbuf(P, "s_lt16", [16, 16], F32)
        s_ones = sbuf(P, "s_ones", [128, 128], F32)
        s_id8 = sbuf(P, "s_id8", [8, 8], F32)
        s_iota8 = sbuf(P, "s_iota8", [128, NEXP], F32)
        s_iorev = sbuf(P, "s_iorev", [128, CH], F32)
        s_selc = sbuf(P, "s_selc", [8, 1], F32)
        s_selp = sbuf(P, "s_selp", [8, 2], F32)
        s_wg = sbuf(P, "s_wg", [128, KC * NEXP], F32)
        s_indA = sbuf(P, "s_indA", [128, KC * H], F32)
        s_indB = sbuf(P, "s_indB", [16, D], F32)
        s_neg1 = sbuf(P, "s_neg1", [128, 1], F32)
        s_n512 = sbuf(P, "s_n512", [128, 1], F32)
        s_wobc = sbuf(P, "s_wobc", [128, KC], F32)
        s_tcol = sbuf(P, "s_tcol", [128, 8], F32)
        s_wobr = sbuf(P, "s_wobr", [1, D], F32)
        s_carry = sbuf(P, "s_carry", [128, KC], F32)
        s_seed = sbuf(P, "s_seed", [128, KC], F32)
        s_lastc = sbuf(P, "s_lastc", [128, KC], F32)
        s_M32 = [sbuf(P, f"s_M32{i}", [128, NEXP], F32) for i in range(KC)]
        s_bb = sbuf(P, "s_bb", [8, 1], F32)
        s_csb = sbuf(P, "s_csb", [128, NEXP], F32)
        s_R1 = sbuf(P, "s_R1", [8, TOK], F32)
        s_lg = sbuf(P, "s_lg", [8, TOK], F32)
        s_mn = sbuf(P, "s_mn", [128, NT], F32)
        s_rstd = sbuf(P, "s_rstd", [128, NT], F32)
        s_oh = sbuf(P, "s_oh", [128, 128], F32)
        s_oh2 = sbuf(P, "s_oh2", [128, 128], F32)
        s_rk1 = sbuf(P, "s_rk1", [128, 128], F32)
        s_rk2 = sbuf(P, "s_rk2", [128, 128], F32)
        s_rt = sbuf(P, "s_rt", [128, 16], F32)
        s_tt16 = sbuf(P, "s_tt16", [16, 8], F32)
        s_tt16b = sbuf(P, "s_tt16b", [16, 8], F32)
        s_cnt8 = sbuf(P, "s_cnt8", [8, 2], F32)
        s_cnta = sbuf(P, "s_cnta", [8, 16], F32)
        s_offr = sbuf(P, "s_offr", [1, 16], F32)
        s_off = sbuf(P, "s_off", [128, 16], F32)
        s_tokid = sbuf(P, "s_tokid", [128, NT], F32)
        s_ids = sbuf(P, "s_ids", [128, 2 * NT], I32)
        s_idf = sbuf(P, "s_idf", [128, 3 * NT], F32)
        s_pair = sbuf(P, "s_pair", [128, 4 * NT], F32)
        s_ii = sbuf(P, "s_ii", [128, 8], I32)
        s_agc = sbuf(P, "s_agc", [8, KC * 128], F32)
        s_trow = sbuf(P, "s_trow", [1, 128], F32)
        s_trow2 = sbuf(P, "s_trow2", [1, 128], F32)
        s_mask8 = sbuf(P, "s_mask8", [128, 8], F32)
        s_sel16 = sbuf(P, "s_sel16", [128, 16], F32)
        s_m8s = sbuf(P, "s_m8s", [128, 8], F32)

        with ExitStack() as B1:
            s_wb = [sbuf(B1, f"s_wb{i}", [128, 4 * D], BF16) for i in range(KC)]
            s_x = [sbuf(B1, f"s_x{i}", [128, CH + 1], F32) for i in range(KC)]
            s_xma = [sbuf(B1, f"s_xma{i}", [128, CH], BF16) for i in range(KC)]
            s_xmb = [sbuf(B1, f"s_xmb{i}", [128, CH], BF16) for i in range(KC)]
            s_xmc = [sbuf(B1, f"s_xmc{i}", [128, CH], BF16) for i in range(KC)]
            s_xmd = [sbuf(B1, f"s_xmd{i}", [128, CH], BF16) for i in range(KC)]
            s_kv = [sbuf(B1, f"s_kv{i}", [128, CH], F32) for i in range(KC)]
            s_sg = [sbuf(B1, f"s_sg{i}", [128, CH], BF16) for i in range(KC)]
            s_rb = [sbuf(B1, f"s_rb{i}", [128, CH], BF16) for i in range(KC)]
            s_kscr = sbuf(B1, "s_kscr", [128, CH], F32)
            s_kscr2 = sbuf(B1, "s_kscr2", [128, CH], F32)
            s_ms = [sbuf(B1, f"s_ms{i}", [128, CH], F32) for i in range(4)]
            s_wrev = sbuf(B1, "s_wrev", [128, CH], F32)
            s_wrev2 = sbuf(B1, "s_wrev2", [128, CH], F32)
            s_cpart = sbuf(B1, "s_cpart", [128, 1], F32)
            s_zero = sbuf(B1, "s_zero", [128, 2048], BF16)

            for k in range(KC):
                xlt0 = dma(Q, s_x[k][:],
                           dxch[k * 128:(k + 1) * 128, 0:CH + 1], aux="qx")
            for p in "gkvr":
                dma(Q, s_bias[p][:], dbias[p])
                dma(Q, s_mix[p][:], dmixv[p])
            dma(Q, s_dec[:], dwdec)
            dma(Q, s_u[:], du)
            dma(Q, s_lt128[:], dlt128)
            dma(Q, s_lt16[:], dlt16)
            dma(Q, s_ones[:], dones)
            dma(Q, s_id8[:], did8)
            dma(Q, s_iota8[:], diota8)
            dma(Q, s_iorev[:], diorev)
            dma(Q, s_selc[:], dselc)
            dma(Q, s_selp[:], dselp)
            dma(Q, s_tokid[:], dtokid)
            dma(Q, s_mask8[:], dmask8)
            dma(Q, s_sel16[:], dsel16)
            dma(Q, s_tbix[:], dtabix)
            for k in range(KC):
                dma(Q, s_wg[:, k * NEXP:(k + 1) * NEXP], dwg[k * 128:(k + 1) * 128, :])
                dma(Q, s_indA[:, k * H:(k + 1) * H], dindA[k * 128:(k + 1) * 128, :])
            dma(Q, s_indB[:], dindB)
            dma(Q, s_wobc[:], dwobc)
            t_wobr = dma(Q, s_wobr[:], dwob)
            ld0 = Q.flush()

            V.op(lambda e: e.memset(s_eps[:], EPS))
            V.op(lambda e: e.memset(s_mhalf[:], -0.5))
            V.op(lambda e: e.memset(s_neg1[:], -1.0))
            V.op(lambda e: e.memset(s_n512[:], -512.0))
            t_z = V.op(lambda e: e.memset(s_zero[:], 0.0))

            t_cz = V.op(lambda e: e.memset(s_carry[:], 0.0))
            for p in "gkvr":
                V.op(lambda e, p=p: e.tensor_scalar(
                    s_om[p][:], s_mix[p][:], -1.0, 1.0, OP.mult, OP.add),
                    deps=[ld0])
            t_consts = V.tok()
            t_e = S.op(lambda e: e.activation(s_e[:], s_dec[:], AF.Exp),
                       deps=[ld0, t_consts])
            t_d = S.op(lambda e: e.activation(s_dec[:], s_e[:], AF.Exp,
                                              scale=s_neg1[:, 0:1]), deps=[t_e])
            t_d512 = S.op(lambda e: e.activation(s_d512[:, 0:KC], s_e[:], AF.Exp,
                                                 scale=s_n512[:, 0:1]), deps=[t_d])
            tprev = t_d512
            for kp in range(1, 3):
                tprev = V.op(lambda e, kp=kp: e.tensor_mul(
                    s_d512[:, kp * KC:(kp + 1) * KC],
                    s_d512[:, (kp - 1) * KC:kp * KC], s_d512[:, 0:KC]),
                    deps=[tprev])
            t_d512a = tprev
            wrev_free = {0: None, 1: None}

            for ko in range(KC):
                T.op(lambda e, ko=ko: e.matmul(
                    pb[5][0:NEXP, 0:1], s_wg[:, ko * NEXP:(ko + 1) * NEXP],
                    s_wobc[:, ko:ko + 1], start=(ko == 0), stop=(ko == KC - 1)),
                    deps=[ld0] if ko == 0 else ())
            t_bb = V.op(lambda e: e.tensor_copy(s_bb[:], pb[5][0:NEXP, 0:1]),
                        deps=[T.tok()])
            for ko in range(KC):
                T.op(lambda e, ko=ko: e.matmul(
                    pb[4][:, 0:NEXP], s_ones[:], s_wg[:, ko * NEXP:(ko + 1) * NEXP],
                    start=(ko == 0), stop=(ko == KC - 1)),
                    deps=[t_bb] if ko == 0 else ())
            t_csb = V.op(lambda e: e.tensor_copy(s_csb[:], pb[4][:, 0:NEXP]),
                         deps=[T.tok()])

            wld = {}
            for wi, p in enumerate("kvgr"):
                for k in range(KC):
                    dma(G, s_wb[k][:, wi * D:(wi + 1) * D],
                        dwT[p][k * 128:(k + 1) * 128, :])
                f_w = G.flush()
                for k in range(KC):
                    wld[(p, k)] = f_w
            for i in range(64):
                dma(G, rs_in[i * 256:(i + 1) * 256, :]
                    .rearrange("(p a) f -> p (a f)", p=128),
                    s_zero[:, 0:2048], deps=[t_z])
            dma(G, tab_in.rearrange("(p a) c -> p (a c)", p=128),
                s_zero[:, 0:640], deps=[t_z])
            ZF = G.tok()

            x_free = {k: None for k in range(KC)}
            spill_tok = {}
            f_sp = {}
            f_spb = {}
            bank_free = {b: None for b in range(6)}
            r1_free = t_csb
            scratch_free = {0: None, 1: None, 2: None, 3: None}
            kscr_free = {0: None, 1: None}
            kmm_prev = None
            grmm_done = None
            mixdump = {}

            def emit_mix(p, dst, k, xl, extra=()):
                si = k % 4
                scr = s_ms[si]
                t1 = S.op(lambda e, k=k, p=p, scr=scr: e.activation(
                    scr[:], s_x[k][:, 0:CH], AF.Identity,
                    scale=s_om[p][:, k:k + 1]),
                    deps=[xl[k], t_consts, scratch_free[si]] + list(extra))
                t2 = V.op(lambda e, k=k, p=p, dst=dst, scr=scr:
                          e.scalar_tensor_tensor(
                              dst[k][:], s_x[k][:, 1:CH + 1],
                              s_mix[p][:, k:k + 1], scr[:],
                              OP.mult, OP.add), deps=[t1])
                scratch_free[si] = t2
                return t2

            def load_x(cn, dep):
                for k in range(KC):
                    xlt = dma(Q, s_x[k][:],
                              dxch[k * 128:(k + 1) * 128,
                                   cn * CH:cn * CH + CH + 1],
                              deps=list(dep), aux="qx")
                return xlt
            xl_cur = xlt0
            for c in range(NCH):
                xl = {k: xl_cur for k in range(KC)}
                for k in range(KC):
                    T.op(lambda e, k=k: e.matmul(
                        pb[7][0:NEXP, 0:CH], s_wg[:, k * NEXP:(k + 1) * NEXP],
                        s_x[k][:, 1:CH + 1], start=(k == 0), stop=(k == KC - 1)),
                        deps=[xl[k]] + ([r1_free] if k == 0 else []))
                r1_free = V.op(lambda e, c=c: e.tensor_copy(
                    s_R1[:, c * CH:(c + 1) * CH], pb[7][0:NEXP, 0:CH]),
                    deps=[T.tok()])

                mixk = [emit_mix("k", s_xma, k, xl, extra=[kmm_prev])
                        for k in range(KC)]
                mixv = [emit_mix("v", s_xmb, k, xl, extra=[kmm_prev])
                        for k in range(KC)]
                mix_done = V.tok()
                xl_cur = load_x((c + 1) % NCH, [mix_done, r1_free])

                ev_kv = {}
                for m in range(KC):
                    bk, bv = m % 3, 3 + m % 3
                    for k in range(KC):
                        T.op(lambda e, k=k, m=m, bk=bk: e.matmul(
                            pb[bk][:, 0:CH],
                            s_wb[k][:, m * 128:(m + 1) * 128],
                            s_xma[k][:], start=(k == 0), stop=(k == KC - 1)),
                            deps=[mixk[k], wld[("k", k)]] +
                                 ([bank_free[bk]] if k == 0 else []))
                    kscr = (s_kscr, s_kscr2)[m % 2]
                    t_k = S.op(lambda e, m=m, bk=bk, kscr=kscr: e.activation(
                        kscr[:], pb[bk][:, 0:CH], AF.Identity,
                        bias=s_bias["k"][:, m:m + 1]),
                        deps=[T.tok(), kscr_free[m % 2]])
                    bank_free[bk] = t_k
                    for k in range(KC):
                        T.op(lambda e, k=k, m=m, bv=bv: e.matmul(
                            pb[bv][:, 0:CH],
                            s_wb[k][:, D + m * 128:D + (m + 1) * 128],
                            s_xmb[k][:], start=(k == 0), stop=(k == KC - 1)),
                            deps=[mixv[k], wld[("v", k)]] +
                                 ([bank_free[bv]] if k == 0 else []))
                    t_kv = V.op(lambda e, m=m, bv=bv, kscr=kscr:
                                e.scalar_tensor_tensor(
                        s_kv[m][:], pb[bv][:, 0:CH], s_bias["v"][:, m:m + 1],
                        kscr[:], OP.add, OP.mult),
                        deps=[T.tok(), t_k, f_sp.get(c - 1)])
                    bank_free[bv] = t_kv
                    kscr_free[m % 2] = t_kv
                    ev_kv[m] = t_kv
                kmm_prev = T.tok()

                for k in range(KC):
                    spill_tok[(c, k, "kv")] = dma(
                        Q, kv_sp[k, :, c * CH:(c + 1) * CH], s_kv[k][:],
                        deps=[ev_kv[k]])
                    if DBG2:
                        spill_tok[(c, k, "kv")] = dma(
                            G, dkvd[k * 128:(k + 1) * 128, c * CH:(c + 1) * CH],
                            s_kv[k][:], deps=[ev_kv[k]])
                    wrv = (s_wrev, s_wrev2)[k % 2]
                    kscr_d = (s_kscr, s_kscr2)[k % 2]
                    tw = S.op(lambda e, k=k, wrv=wrv: e.activation(
                        wrv[:], s_iorev[:], AF.Exp, scale=s_e[:, k:k + 1]),
                        deps=[t_d512a, wrev_free[k % 2]])
                    tp = V.op(lambda e, k=k, wrv=wrv, kscr_d=kscr_d:
                              e.scalar_tensor_tensor(
                        kscr_d[:], wrv[:], 1.0,
                        s_kv[k][:], OP.mult, OP.mult,
                        accum_out=s_cpart[:, 0:1]),
                        deps=[tw, ev_kv[k], t_cz])
                    wrev_free[k % 2] = tp
                    kscr_free[k % 2] = tp
                    if c < NCH - 1:
                        pw = (NCH - 2 - c) * KC + k
                        V.op(lambda e, pw=pw, k=k: e.scalar_tensor_tensor(
                            s_carry[:, k:k + 1], s_cpart[:, 0:1],
                            s_d512[:, pw:pw + 1], s_carry[:, k:k + 1],
                            OP.mult, OP.add), deps=[tp])
                    else:
                        V.op(lambda e, k=k: e.tensor_add(
                            s_carry[:, k:k + 1], s_carry[:, k:k + 1],
                            s_cpart[:, 0:1]), deps=[tp])
                    x_free[k] = mixv[k]
                f_sp[c] = Q.flush()

            t_call = V.tok()
            for k in range(KC):
                dma(Q, carry_in[k:k + 1, :], s_carry[:, k:k + 1], deps=[t_call])
            f_car = Q.flush()
            tok_cag = coll("AllGather", OP.bypass, [carry_in], [carry_ag], cag,
                           deps=[f_car, ZF])

            for c in range(NCH):
                xl = {k: xl_cur for k in range(KC)}
                mixg = [emit_mix("g", s_xmc, k, xl, extra=[grmm_done])
                        for k in range(KC)]
                mixr = [emit_mix("r", s_xmd, k, xl, extra=[grmm_done])
                        for k in range(KC)]
                mixb_done = V.tok()
                if c + 1 < NCH:
                    xl_cur = load_x(c + 1, [mixb_done])
                ev_g, ev_r = {}, {}
                for m in range(KC):
                    bg, br = m % 3, 3 + m % 3
                    for k in range(KC):
                        T.op(lambda e, k=k, m=m, bg=bg: e.matmul(
                            pb[bg][:, 0:CH],
                            s_wb[k][:, 2 * D + m * 128:2 * D + (m + 1) * 128],
                            s_xmc[k][:], start=(k == 0), stop=(k == KC - 1)),
                            deps=[mixg[k], wld[("g", k)]] +
                                 ([bank_free[bg]] if k == 0 else []))
                    ev_g[m] = S.op(lambda e, m=m, bg=bg: e.activation(
                        s_sg[m][:], pb[bg][:, 0:CH], AF.Sigmoid,
                        bias=s_bias["g"][:, m:m + 1]),
                        deps=[T.tok(), f_spb.get(c - 1)])
                    bank_free[bg] = ev_g[m]
                    for k in range(KC):
                        T.op(lambda e, k=k, m=m, br=br: e.matmul(
                            pb[br][:, 0:CH],
                            s_wb[k][:, 3 * D + m * 128:3 * D + (m + 1) * 128],
                            s_xmd[k][:], start=(k == 0), stop=(k == KC - 1)),
                            deps=[mixr[k], wld[("r", k)]] +
                                 ([bank_free[br]] if k == 0 else []))
                    ev_r[m] = S.op(lambda e, m=m, br=br: e.activation(
                        s_rb[m][:], pb[br][:, 0:CH], AF.Identity,
                        bias=s_bias["r"][:, m:m + 1]),
                        deps=[T.tok(), f_spb.get(c - 1)])
                    bank_free[br] = ev_r[m]
                grmm_done = T.tok()
                for k in range(KC):
                    dma(Q, sg_sp[k, :, c * CH:(c + 1) * CH], s_sg[k][:],
                        deps=[ev_g[k]])
                    dma(Q, rb_sp[k, :, c * CH:(c + 1) * CH], s_rb[k][:],
                        deps=[ev_r[k]])
                    x_free[k] = mixr[k]
                f_spb[c] = Q.flush()

            f_b1 = Q.flush()
            run_block(B1)

        with ExitStack() as B2:
            s_kvA = [sbuf(B2, f"s_kvA{i}", [128, CH], F32) for i in range(KC)]
            s_kvB = [sbuf(B2, f"s_kvB{i}", [128, CH], F32) for i in range(KC)]
            s_sgA = [sbuf(B2, f"s_sgA{i}", [128, CH], BF16) for i in range(KC)]
            s_sgB = [sbuf(B2, f"s_sgB{i}", [128, CH], BF16) for i in range(KC)]
            s_rbA = [sbuf(B2, f"s_rbA{i}", [128, CH], BF16) for i in range(KC)]
            s_rbB = [sbuf(B2, f"s_rbB{i}", [128, CH], BF16) for i in range(KC)]
            s_acc = [sbuf(B2, f"s_acc{i}", [128, CH], F32) for i in range(KC)]
            s_wk = [sbuf(B2, f"s_wk{i}", [128, CH], F32) for i in range(KC)]
            s_opb = [sbuf(B2, f"s_opb{i}", [128, CH], BF16) for i in range(KC)]
            s_opf = [sbuf(B2, f"s_opf{i}", [128, CH], F32) for i in range(KC)]
            s_woT = [sbuf(B2, f"s_woT{i}", [128, D], BF16) for i in range(KC)]
            s_won = [sbuf(B2, f"s_won{i}", [128, 128], F32) for i in range(2)]
            s_sq = sbuf(B2, "s_sq", [128, CH], F32)
            s_l1 = sbuf(B2, "s_l1", [128, CH], F32)
            s_l2 = sbuf(B2, "s_l2", [128, CH], F32)
            s_hl = sbuf(B2, "s_hl", [16, 2 * CH], F32)
            s_xt = [sbuf(B2, f"s_xt{i}", [128, D], F32) for i in range(2)]
            s_xr2 = [sbuf(B2, f"s_xr2{i}", [128, D], F32) for i in range(2)]
            s_x1 = sbuf(B2, "s_x1t", [128, D], F32)
            s_x1b = sbuf(B2, "s_x1bt", [128, D], BF16)

            dma(Q, s_agc[:], carry_ag, deps=[tok_cag])
            t_agl = Q.flush()
            for k in range(KC):
                T.op(lambda e, k=k: e.matmul(
                    pb[0][:, k:k + 1], s_agc[0:8, k * 128:(k + 1) * 128],
                    s_selc[:], start=True, stop=True), deps=[t_agl])
            t_seed = V.op(lambda e: e.tensor_copy(s_seed[:], pb[0][:, 0:KC]),
                          deps=[T.tok()])

            t_m32 = []
            mm_toks = [t_agl, t_agl]
            ii = 0
            for mc in range(KC):
                for ko in range(KC):
                    par = ii % 2
                    lw = dma(Q, s_won[par][:],
                             dwon[ko * 128:(ko + 1) * 128, mc * 128:(mc + 1) * 128],
                             deps=[mm_toks[par]], aux="qm32" if par == 0 else "qm32b")
                    T.op(lambda e, ko=ko, par=par: e.matmul(
                        pb[1][0:128, 0:NEXP], s_won[par][:],
                        s_wg[:, ko * NEXP:(ko + 1) * NEXP],
                        start=(ko == 0), stop=(ko == KC - 1)),
                        deps=[lw] + ([t_m32[-1]] if ko == 0 and t_m32 else []))
                    mm_toks[par] = T.tok()
                    ii += 1
                t_m32.append(V.op(lambda e, mc=mc: e.tensor_copy(
                    s_M32[mc][:], pb[1][:, 0:NEXP]), deps=[T.tok()]))

            for k in range(KC):
                dma(G, s_woT[k][:], dwoT[k * 128:(k + 1) * 128, :])
            f_wot = G.flush()
            wotl = {k: f_wot for k in range(KC)}

            SETS = {0: (s_kvA, s_sgA, s_rbA), 1: (s_kvB, s_sgB, s_rbB)}
            pre = {}
            for k in range(KC):
                dma(Q, s_kvA[k][:], kv_sp[k, :, 0:CH], deps=[f_b1], aux="qds0")
                dma(Q, s_sgA[k][:], sg_sp[k, :, 0:CH], deps=[f_b1], aux="qds0")
                t0 = dma(Q, s_rbA[k][:], rb_sp[k, :, 0:CH], deps=[f_b1],
                         aux="qds0")
            for k in range(KC):
                for w in ("kv", "sg", "rb"):
                    pre[(0, k, w)] = t0

            bank_free = {i: None for i in range(8)}
            bank_free[0] = t_seed
            bank_free[1] = t_m32[-1]
            accdumps = {}
            lg_done = {}
            bank7_free = [None]
            t_rw_all = {}
            guard_T = {}
            guard_S = {}
            prev_chunk_V = None
            prev_chunk_T = None
            x1_exports = []
            x1sp_tok = {}
            xfe_tok = {}
            xt_consumed = {}
            xp_tok = {}

            def emit_router_tile(ti):
                T.op(lambda e, ti=ti: e.transpose(
                    pb[7][:, 0:8], s_lg[0:8, ti * 128:(ti + 1) * 128], s_id8[:]),
                    deps=[lg_done[ti], bank7_free[0]])
                ttr = T.tok()
                ta = V.op(lambda e, ti=ti: e.scalar_tensor_tensor(
                    s_rt[:, 0:8], s_csb[:], s_mn[:, ti:ti + 1], pb[7][:, 0:8],
                    OP.mult, OP.add), deps=[ttr])
                bank7_free[0] = ta
                V.op(lambda e, ti=ti: e.tensor_scalar_mul(
                    s_rt[:, 0:8], s_rt[:, 0:8], s_rstd[:, ti:ti + 1]), deps=[ta])
                V.op(lambda e: e.reduce_max(s_rt[:, 8:9], s_rt[:, 0:8],
                                            axis=AX.X))
                V.op(lambda e: e.tensor_scalar(
                    s_col[:, 0:8], s_rt[:, 0:8], s_rt[:, 8:9], None,
                    OP.is_equal))
                V.op(lambda e: e.tensor_tensor_scan(
                    s_idf[:, 0:8], s_col[:, 0:8], s_col[:, 0:8], 0.0,
                    OP.add, OP.bypass))
                V.op(lambda e: e.tensor_scalar(
                    s_idf[:, 0:8], s_idf[:, 0:8], 1.0, None, OP.is_equal))
                V.op(lambda e, ti=ti: e.tensor_mul(
                    s_oh[:, ti * 8:(ti + 1) * 8], s_col[:, 0:8], s_idf[:, 0:8]))
                V.op(lambda e, ti=ti: e.scalar_tensor_tensor(
                    s_col[:, 0:8], s_oh[:, ti * 8:(ti + 1) * 8], -1e30,
                    s_rt[:, 0:8], OP.mult, OP.add))
                V.op(lambda e: e.reduce_max(s_rt[:, 10:11], s_col[:, 0:8],
                                            axis=AX.X))
                V.op(lambda e: e.tensor_scalar(
                    s_col[:, 0:8], s_col[:, 0:8], s_rt[:, 10:11], None,
                    OP.is_equal))
                V.op(lambda e: e.tensor_tensor_scan(
                    s_idf[:, 0:8], s_col[:, 0:8], s_col[:, 0:8], 0.0,
                    OP.add, OP.bypass))
                V.op(lambda e: e.tensor_scalar(
                    s_idf[:, 0:8], s_idf[:, 0:8], 1.0, None, OP.is_equal))
                tf2 = V.op(lambda e, ti=ti: e.tensor_mul(
                    s_oh2[:, ti * 8:(ti + 1) * 8], s_col[:, 0:8],
                    s_idf[:, 0:8]))
                te = S.op(lambda e: e.activation(
                    s_rt[:, 12:13], s_rt[:, 8:9], AF.Exp, scale=s_neg1[:, 0:1],
                    bias=s_rt[:, 10:11]), deps=[tf2])
                V.op(lambda e: e.tensor_scalar_add(
                    s_rt[:, 13:14], s_rt[:, 12:13], 1.0), deps=[te])
                V.op(lambda e: e.reciprocal(s_rt[:, 14:15], s_rt[:, 13:14]))
                tp2 = V.op(lambda e: e.tensor_mul(
                    s_rt[:, 15:16], s_rt[:, 12:13], s_rt[:, 14:15]))
                V.op(lambda e, ti=ti: e.tensor_copy(
                    s_pair[:, 4 * ti:4 * ti + 1], s_tokid[:, ti:ti + 1]))
                V.op(lambda e, ti=ti: e.tensor_copy(
                    s_pair[:, 4 * ti + 1:4 * ti + 2], s_rt[:, 14:15]))
                V.op(lambda e, ti=ti: e.tensor_copy(
                    s_pair[:, 4 * ti + 2:4 * ti + 3], s_tokid[:, ti:ti + 1]))
                V.op(lambda e, ti=ti: e.tensor_copy(
                    s_pair[:, 4 * ti + 3:4 * ti + 4], s_rt[:, 15:16]))
                V.op(lambda e, ti=ti: e.scalar_tensor_tensor(
                    s_col[:, 0:8], s_oh[:, ti * 8:(ti + 1) * 8], 1.0,
                    s_iota8[:], OP.mult, OP.mult,
                    accum_out=s_idf[:, 16 + ti:16 + ti + 1]))
                V.op(lambda e, ti=ti: e.scalar_tensor_tensor(
                    s_col[:, 0:8], s_oh2[:, ti * 8:(ti + 1) * 8], 1.0,
                    s_iota8[:], OP.mult, OP.mult,
                    accum_out=s_idf[:, 32 + ti:32 + ti + 1]))

            def emit_scan_k(cs, k):
                ckv, csg, crb = SETS[cs % 2]
                seed_ap = s_seed if cs == 0 else s_lastc
                ts = V.op(lambda e, k=k, seed_ap=seed_ap, ckv=ckv:
                          e.tensor_tensor_scan(
                              s_acc[k][:],
                              s_dec[:, k:k + 1].broadcast_to((128, CH)),
                              ckv[k][:],
                              seed_ap[:, k:k + 1], OP.mult, OP.add),
                          deps=[pre[(cs, k, "kv")], guard_T.get(cs - 1),
                                guard_S.get(cs - 1),
                                t_seed if cs == 0 else None])
                tl = V.op(lambda e, k=k: e.tensor_copy(
                    s_lastc[:, k:k + 1], s_acc[k][:, CH - 1:CH]), deps=[ts])
                t_wkv = V.op(lambda e, k=k, ckv=ckv: e.scalar_tensor_tensor(
                    s_wk[k][:], ckv[k][:], s_u[:, k:k + 1], s_acc[k][:],
                    OP.mult, OP.add), deps=[tl])
                t_rw_all[(cs, k)] = V.op(
                    lambda e, k=k, crb=crb: e.tensor_mul(
                        s_acc[k][:], s_wk[k][:], crb[k][:]),
                    deps=[t_wkv, pre[(cs, k, "rb")]])

            for k in range(KC):
                emit_scan_k(0, k)
            xq_free = {0: None, 1: None}
            for c in range(NCH):
                ckv, csg, crb = SETS[c % 2]
                if c + 1 < NCH:
                    for k in range(KC):
                        aux = "qds0" if (c + 1) % 2 == 0 else "qds1"
                        nkv, nsg, nrb = SETS[(c + 1) % 2]
                        dma(Q, nkv[k][:], kv_sp[k, :, (c + 1) * CH:(c + 2) * CH],
                            deps=[prev_chunk_V, f_b1], aux=aux)
                        dma(Q, nsg[k][:], sg_sp[k, :, (c + 1) * CH:(c + 2) * CH],
                            deps=[prev_chunk_V, f_b1], aux=aux)
                        t0 = dma(Q, nrb[k][:],
                                 rb_sp[k, :, (c + 1) * CH:(c + 2) * CH],
                                 deps=[prev_chunk_V, f_b1], aux=aux)
                    for k in range(KC):
                        for w in ("kv", "sg", "rb"):
                            pre[(c + 1, k, w)] = t0
                for k in range(KC):
                    T.op(lambda e, k=k: e.matmul(
                        pb[0][0:16, 0:CH], s_indA[:, k * H:(k + 1) * H],
                        s_acc[k][:], start=(k == 0), stop=(k == KC - 1)),
                        deps=[t_rw_all[(c, k)]] +
                             ([bank_free[0]] if k == 0 else []))
                sq_mm = {}
                for k in range(KC):
                    scr = s_sq if k % 2 == 0 else s_l2
                    tsq = S.op(lambda e, k=k, scr=scr: e.activation(
                        scr[:], s_acc[k][:], AF.Square),
                        deps=[t_rw_all[(c, k)], sq_mm.get(k - 2)])
                    T.op(lambda e, k=k, scr=scr: e.matmul(
                        pb[1][0:16, 0:CH], s_indA[:, k * H:(k + 1) * H],
                        scr[:], start=(k == 0), stop=(k == KC - 1)),
                        deps=[tsq] + ([bank_free[1]] if k == 0 else []))
                    sq_mm[k] = T.tok()
                guard_T[c] = T.tok()
                guard_S[c] = S.tok()
                stats_done = T.tok()
                tm = V.op(lambda e: e.tensor_copy(s_hl[:, 0:CH], pb[0][0:16, 0:CH]),
                          deps=[stats_done])
                bank_free[0] = tm
                tv = V.op(lambda e: e.tensor_mul(
                    s_hl[:, CH:2 * CH], s_hl[:, 0:CH], s_hl[:, 0:CH]), deps=[tm])
                tv2 = V.op(lambda e: e.scalar_tensor_tensor(
                    s_hl[:, CH:2 * CH], s_hl[:, CH:2 * CH], -1.0,
                    pb[1][0:16, 0:CH], OP.mult, OP.add), deps=[tv])
                bank_free[1] = tv2
                tr1 = S.op(lambda e: e.activation(
                    s_hl[:, CH:2 * CH], s_hl[:, CH:2 * CH], AF.Ln,
                    bias=s_eps[0:16, 0:1]), deps=[tv2])
                tr2 = S.op(lambda e: e.activation(
                    s_hl[:, CH:2 * CH], s_hl[:, CH:2 * CH], AF.Exp,
                    scale=s_mhalf[0:16, 0:1]), deps=[tr1])
                t_opf = {}
                for k in range(KC):
                    bm = (2, 0)[k % 2]
                    brr = (3, 1)[k % 2]
                    T.op(lambda e, k=k, bm=bm: e.matmul(
                        pb[bm][:, 0:CH], s_indB[:, k * 128:(k + 1) * 128],
                        s_hl[:, 0:CH], start=True, stop=True),
                        deps=[tm, bank_free[bm]])
                    tmb = T.tok()
                    T.op(lambda e, k=k, brr=brr: e.matmul(
                        pb[brr][:, 0:CH], s_indB[:, k * 128:(k + 1) * 128],
                        s_hl[:, CH:2 * CH], start=True, stop=True),
                        deps=[tr2, bank_free[brr]])
                    trb = T.tok()
                    t1 = V.op(lambda e, k=k, bm=bm: e.scalar_tensor_tensor(
                        s_l1[:], s_acc[k][:], 1.0, pb[bm][:, 0:CH],
                        OP.mult, OP.subtract), deps=[tmb])
                    bank_free[bm] = t1
                    t2 = V.op(lambda e, brr=brr: e.scalar_tensor_tensor(
                        s_l2[:], s_l1[:], 1.0, pb[brr][:, 0:CH],
                        OP.mult, OP.mult), deps=[trb])
                    bank_free[brr] = t2
                    t_opf[k] = V.op(lambda e, k=k, csg=csg: e.tensor_mul(
                        s_opf[k][:], s_l2[:], csg[k][:]),
                        deps=[t2, pre[(c, k, "sg")], prev_chunk_T])
                    S.op(lambda e, k=k: e.activation(
                        s_opb[k][:], s_opf[k][:], AF.Identity),
                        deps=[t_opf[k]])
                opb_done = S.tok()
                for k in range(KC):
                    T.op(lambda e, k=k: e.matmul(
                        pb[6][0:NEXP, 0:CH], s_M32[k][:], s_opf[k][:],
                        start=(k == 0), stop=(k == KC - 1)),
                        deps=[t_opf[k]] + ([bank_free[6], t_m32[-1]]
                                           if k == 0 else []))
                tlg = V.op(lambda e, c=c: e.scalar_tensor_tensor(
                    s_lg[:, c * CH:(c + 1) * CH], pb[6][0:NEXP, 0:CH], 1.0,
                    s_R1[:, c * CH:(c + 1) * CH], OP.mult, OP.add),
                    deps=[T.tok()])
                bank_free[6] = tlg
                tlg2 = V.op(lambda e, c=c: e.tensor_scalar(
                    s_lg[:, c * CH:(c + 1) * CH], s_lg[:, c * CH:(c + 1) * CH],
                    s_bb[0:NEXP, 0:1], None, OP.add), deps=[tlg])
                for tt4 in range(4):
                    lg_done[c * 4 + tt4] = tlg2
                st_info = {}

                def scan_pair(tt):
                    if c + 1 < NCH:
                        emit_scan_k(c + 1, 2 * tt)
                        emit_scan_k(c + 1, 2 * tt + 1)

                def finish_tile(tt):
                    ti = c * 4 + tt
                    pq = tt % 2
                    a = 4 * pq
                    ts2 = st_info[tt]
                    tsum = V.op(lambda e, a=a: e.tensor_add(
                        s_tcol[:, a:a + 1], s_tcol[:, a:a + 1],
                        s_tcol[:, a + 1:a + 2]), deps=[ts2])
                    tmn = V.op(lambda e, ti=ti, a=a: e.tensor_scalar_mul(
                        s_mn[:, ti:ti + 1], s_tcol[:, a:a + 1], -1.0 / D),
                        deps=[tsum])
                    q1 = S.op(lambda e, ti=ti, pq=pq, a=a: e.activation(
                        s_sq[:, 0:CH], s_xr2[pq][:, 0:512], AF.Square,
                        bias=s_mn[:, ti:ti + 1],
                        accum_out=s_tcol[:, a + 1:a + 2]), deps=[tmn])
                    q2 = S.op(lambda e, ti=ti, pq=pq, a=a: e.activation(
                        s_sq[:, 0:CH], s_xr2[pq][:, 512:1024], AF.Square,
                        bias=s_mn[:, ti:ti + 1],
                        accum_out=s_tcol[:, a + 2:a + 3]), deps=[q1])
                    scan_pair(tt)
                    tva = V.op(lambda e, a=a: e.tensor_add(
                        s_tcol[:, a + 1:a + 2], s_tcol[:, a + 1:a + 2],
                        s_tcol[:, a + 2:a + 3]), deps=[q2])
                    xq_free[pq] = tva
                    tvb = V.op(lambda e, a=a: e.tensor_scalar_mul(
                        s_tcol[:, a + 1:a + 2], s_tcol[:, a + 1:a + 2],
                        1.0 / D), deps=[tva])
                    r1 = S.op(lambda e, a=a: e.activation(
                        s_tcol[:, a + 3:a + 4], s_tcol[:, a + 1:a + 2], AF.Ln,
                        bias=s_eps[:, 0:1]), deps=[tvb])
                    r2 = S.op(lambda e, ti=ti, a=a: e.activation(
                        s_rstd[:, ti:ti + 1], s_tcol[:, a + 3:a + 4], AF.Exp,
                        scale=s_mhalf[:, 0:1]), deps=[r1])
                    xp = "qxp0" if ti % 2 == 0 else "qxp1"
                    xfp = "qxf0" if ti % 2 == 0 else "qxf1"
                    tx1o = V.op(lambda e, ti=ti, pq=pq: e.tensor_scalar(
                        s_x1[:], s_xr2[pq][:], s_mn[:, ti:ti + 1],
                        s_rstd[:, ti:ti + 1], OP.add, OP.mult),
                        deps=[r2, xp_tok.get(ti - 1), xfe_tok.get(ti - 1)])
                    tx1b = V.op(lambda e: e.tensor_copy(s_x1b[:], s_x1[:]),
                                deps=[tx1o])
                    x1sp_tok[ti] = dma(Q, x1_sp[ti * 128:(ti + 1) * 128, :],
                                       s_x1[:], deps=[tx1o], aux=xp)
                    xp_tok[ti] = x1sp_tok[ti]
                    xfe_tok[ti] = dma(Q, xf_in[ti * 128:(ti + 1) * 128, :],
                                      s_x1b[:],
                                      deps=[tx1b, xfe_tok.get(ti - 2)],
                                      aux=xfp)
                    if DBG:
                        xp_tok[ti] = dma(Q, dx1d[ti * 128:(ti + 1) * 128, :],
                                         s_x1[:], deps=[tx1o], aux=xp)

                for tt in range(4):
                    ti = c * 4 + tt
                    pq = tt % 2
                    t_xt = dma(S, s_xt[pq][:],
                               dxtok[ti * 128:(ti + 1) * 128, :],
                               deps=[xt_consumed.get(ti - 2)],
                               aux="sxt0" if ti % 2 == 0 else "sxt1")
                    for half in range(2):
                        bank = 4 + half
                        for k in range(KC):
                            T.op(lambda e, k=k, tt=tt, half=half, bank=bank:
                                 e.matmul(
                                     pb[bank][:, 0:512],
                                     s_opb[k][:, tt * 128:(tt + 1) * 128],
                                     s_woT[k][:, half * 512:(half + 1) * 512],
                                     start=(k == 0), stop=False),
                                 deps=[opb_done, wotl[k]] +
                                      ([bank_free[bank]] if k == 0 else []))
                        T.op(lambda e, half=half, bank=bank: e.matmul(
                            pb[bank][:, 0:512], s_ones[0:1, 0:128],
                            s_wobr[0:1, half * 512:(half + 1) * 512],
                            start=False, stop=True), deps=[t_wobr])
                    wo_done = T.tok()
                    tx1 = V.op(lambda e, pq=pq: e.tensor_add(
                        s_xr2[pq][:, 0:512], pb[4][:, 0:512],
                        s_xt[pq][:, 0:512]),
                        deps=[wo_done, t_xt, xq_free[pq]])
                    bank_free[4] = tx1
                    tx2 = V.op(lambda e, pq=pq: e.tensor_add(
                        s_xr2[pq][:, 512:1024], pb[5][:, 0:512],
                        s_xt[pq][:, 512:1024]), deps=[wo_done])
                    bank_free[5] = tx2
                    xt_consumed[ti] = tx2
                    ts1 = S.op(lambda e, pq=pq, a=4 * pq: e.activation(
                        s_sq[:, 0:CH], s_xr2[pq][:, 0:512], AF.Identity,
                        accum_out=s_tcol[:, a:a + 1]), deps=[tx1, tx2])
                    st_info[tt] = S.op(lambda e, pq=pq, a=4 * pq: e.activation(
                        s_sq[:, 0:CH], s_xr2[pq][:, 512:1024], AF.Identity,
                        accum_out=s_tcol[:, a + 1:a + 2]), deps=[ts1])
                    if tt >= 1:
                        finish_tile(tt - 1)
                        emit_router_tile(c * 4 + tt - 1)
                finish_tile(3)
                emit_router_tile(c * 4 + 3)
                prev_chunk_V = V.tok()
                prev_chunk_T = T.tok()
                if c == 0:
                    for i in range(8):
                        dma(G, cfc_b[i * 128:(i + 1) * 128, :],
                            dcfc[i * 128:(i + 1) * 128, :], aux="gpc")
                    for i in range(8):
                        tcb = dma(G, cpr_b[i * 512:(i + 1) * 512, :],
                                  dcpr[i * 512:(i + 1) * 512, :], aux="gpc")
            f_xf = Q.flush()
            oh_all = V.tok()
            T.op(lambda e: e.matmul(pb[2][0:1, 0:128], s_ones[:, 0:1], s_oh[:],
                                    start=True, stop=True),
                 deps=[oh_all, bank_free[2]])
            tt1m = T.tok()
            T.op(lambda e: e.matmul(pb[3][0:1, 0:128], s_ones[:, 0:1], s_oh2[:],
                                    start=True, stop=True),
                 deps=[oh_all, bank_free[3]])
            tt2m = T.tok()
            tev1 = V.op(lambda e: e.tensor_copy(s_trow[:], pb[2][0:1, 0:128]),
                        deps=[tt1m])
            bank_free[2] = tev1
            tev2 = V.op(lambda e: e.tensor_copy(s_trow2[:], pb[3][0:1, 0:128]),
                        deps=[tt2m])
            bank_free[3] = tev2
            tcc = {}
            for cidx, (srow, sdst, tevx) in enumerate(
                    ((s_trow, s_tt16, tev1), (s_trow2, s_tt16b, tev2))):
                T.op(lambda e, srow=srow, cidx=cidx: e.transpose(
                    pb[cidx][:, 0:1], srow[0:1, 0:128], s_id8[0:1, 0:1]),
                    deps=[tevx, bank_free[cidx]])
                tcl1 = V.op(lambda e, cidx=cidx: e.tensor_copy(
                    s_col[:, 0:1], pb[cidx][:, 0:1]), deps=[T.tok()])
                bank_free[cidx] = tcl1
                tmul = V.op(lambda e: e.tensor_scalar_mul(
                    s_m8s[:], s_mask8[:], s_col[:, 0:1]), deps=[tcl1])
                T.op(lambda e: e.matmul(pb[6][0:16, 0:8], s_sel16[:],
                                        s_m8s[:], start=True, stop=True),
                     deps=[tmul, bank_free[6]])
                tcc[cidx] = V.op(lambda e, sdst=sdst: e.tensor_copy(
                    sdst[:], pb[6][0:16, 0:8]), deps=[T.tok()])
                bank_free[6] = tcc[cidx]
            tl1, tl2 = tcc[0], tcc[1]
            T.op(lambda e: e.matmul(pb[6][0:8, 0:1], s_tt16[:],
                                    s_ones[0:16, 0:1], start=True, stop=True),
                 deps=[tl1, bank_free[6]])
            T.op(lambda e: e.matmul(pb[6][0:8, 1:2], s_tt16b[:],
                                    s_ones[0:16, 0:1], start=False, stop=True),
                 deps=[tl2])
            tcv = V.op(lambda e: e.tensor_copy(s_cnt8[:], pb[6][0:8, 0:2]),
                       deps=[T.tok()])
            bank_free[6] = tcv
            dma(Q, cnt_in[0, 0:8], s_cnt8[0:8, 0:1], deps=[tcv])
            dma(Q, cnt_in[0, 8:16], s_cnt8[0:8, 1:2], deps=[tcv])
            f_ce = Q.flush()
            tok_cnt = coll("AllGather", OP.bypass, [cnt_in], [cnt_ag], ccnt,
                           deps=[f_ce])
            tok_xf = coll("AllGather", OP.bypass, [xf_in], [xf_ag], cxf,
                          deps=[f_xf])
            dma(Q, s_cnta[:], cnt_ag, deps=[tok_cnt])
            tcl = Q.flush()
            T.op(lambda e: e.matmul(pb[0][:, 0:128], s_lt128[:], s_oh[:],
                                    start=True, stop=True),
                 deps=[oh_all, bank_free[0]])
            rk1_grp = T.tok()
            T.op(lambda e: e.matmul(pb[1][:, 0:128], s_lt128[:], s_oh2[:],
                                    start=True, stop=True),
                 deps=[oh_all, bank_free[1]])
            rk2_grp = T.tok()
            T.op(lambda e: e.matmul(pb[2][0:16, 0:8], s_lt16[:], s_tt16[:],
                                    start=True, stop=True),
                 deps=[tl1, bank_free[2]])
            to1 = T.tok()
            T.op(lambda e: e.matmul(pb[3][0:16, 0:8], s_lt16[:], s_tt16b[:],
                                    start=True, stop=True),
                 deps=[tl2, bank_free[3]])
            to2 = T.tok()
            tof1 = V.op(lambda e: e.tensor_copy(s_tt16[:], pb[2][0:16, 0:8]),
                        deps=[to1])
            tof2 = V.op(lambda e: e.tensor_copy(s_tt16b[:], pb[3][0:16, 0:8]),
                        deps=[to2])
            tb3 = dma(Q, toff_b1, s_tt16[:], deps=[tof1])
            tb4 = dma(Q, toff_b2, s_tt16b[:], deps=[tof2])
            f_tb2 = Q.flush()
            dma(Q, s_trow[:], toff_b1.rearrange("t e -> (t e)"), deps=[f_tb2])
            dma(Q, s_trow2[:], toff_b2.rearrange("t e -> (t e)"), deps=[f_tb2])
            f_tlc = Q.flush()
            tl3 = tl4 = f_tlc
            T.op(lambda e: e.matmul(pb[0][:, 0:128], s_ones[0:1, :], s_trow[:],
                                    start=False, stop=True),
                 deps=[tl3, rk1_grp])
            rk1_done = T.tok()
            T.op(lambda e: e.matmul(pb[1][:, 0:128], s_ones[0:1, :], s_trow2[:],
                                    start=False, stop=True),
                 deps=[tl4, rk2_grp])
            rk2_done = T.tok()
            trke1 = V.op(lambda e: e.tensor_copy(s_rk1[:], pb[0][:, 0:128]),
                         deps=[rk1_done])
            bank_free[0] = trke1
            trke2 = V.op(lambda e: e.tensor_copy(s_rk2[:], pb[1][:, 0:128]),
                         deps=[rk2_done])
            bank_free[1] = trke2
            T.op(lambda e: e.matmul(pb[7][0:1, 0:16], s_selp[:, 0:1], s_cnta[:],
                                    start=True, stop=False),
                 deps=[tcl, bank7_free[0]])
            T.op(lambda e: e.matmul(pb[7][0:1, 8:16], s_selp[:, 1:2],
                                    s_cnta[:, 0:8], start=False, stop=True))
            toffr = V.op(lambda e: e.tensor_copy(s_offr[:], pb[7][0:1, 0:16]),
                         deps=[T.tok()])
            T.op(lambda e: e.matmul(pb[7][:, 0:16], s_ones[0:1, :], s_offr[:],
                                    start=True, stop=True), deps=[toffr])
            toffb = V.op(lambda e: e.tensor_copy(s_off[:], pb[7][:, 0:16]),
                         deps=[T.tok()])
            bank7_free[0] = toffb
            bank_free[7] = toffb

            scat_toks = []
            for ti in range(NT):
                for kk in range(2):
                    ohs = s_oh if kk == 0 else s_oh2
                    rks = s_rk1 if kk == 0 else s_rk2
                    rkd = trke1 if kk == 0 else trke2
                    tr = V.op(lambda e, ti=ti, ohs=ohs, rks=rks:
                              e.scalar_tensor_tensor(
                                  s_col[:, 0:8], ohs[:, ti * 8:(ti + 1) * 8],
                                  1.0, rks[:, ti * 8:(ti + 1) * 8],
                                  OP.mult, OP.mult, accum_out=s_rt[:, 0:1]),
                              deps=[rkd, toffb])
                    V.op(lambda e, ti=ti, kk=kk, ohs=ohs: e.scalar_tensor_tensor(
                        s_col[:, 0:8], ohs[:, ti * 8:(ti + 1) * 8], 1.0,
                        s_off[:, kk * 8:(kk + 1) * 8], OP.mult, OP.mult,
                        accum_out=s_rt[:, 1:2]), deps=[tr])
                    V.op(lambda e: e.tensor_add(s_rt[:, 2:3], s_rt[:, 0:1],
                                                s_rt[:, 1:2]))
                    V.op(lambda e: e.tensor_copy(s_ii[:, 0:1], s_rt[:, 2:3]))
                    V.op(lambda e, ti=ti, kk=kk: e.tensor_copy(
                        s_ii[:, 1:2],
                        s_idf[:, 16 * (kk + 1) + ti:16 * (kk + 1) + ti + 1]))
                    V.op(lambda e: e.tensor_scalar(
                        s_rt[:, 3:4], s_rt[:, 2:3], float(CAP), None, OP.is_lt))
                    V.op(lambda e: e.tensor_copy(s_ii[:, 2:3], s_rt[:, 3:4]))
                    V.op(lambda e: e.tensor_scalar(
                        s_ii[:, 3:4], s_ii[:, 0:1], 127, None, OP.bitwise_and))
                    V.op(lambda e: e.tensor_scalar(
                        s_ii[:, 3:4], s_ii[:, 3:4], 40, None, OP.mult))
                    V.op(lambda e: e.tensor_scalar(
                        s_ii[:, 4:5], s_ii[:, 0:1], 7, None,
                        OP.arith_shift_right))
                    V.op(lambda e: e.tensor_add(s_ii[:, 3:4], s_ii[:, 3:4],
                                                s_ii[:, 4:5]))
                    V.op(lambda e: e.tensor_scalar(
                        s_ii[:, 1:2], s_ii[:, 1:2], CAP, None, OP.mult))
                    V.op(lambda e: e.tensor_add(s_ii[:, 3:4], s_ii[:, 3:4],
                                                s_ii[:, 1:2]))
                    V.op(lambda e: e.tensor_scalar(
                        s_ii[:, 2:3], s_ii[:, 2:3], -1, None, OP.mult))
                    V.op(lambda e: e.tensor_scalar(
                        s_ii[:, 2:3], s_ii[:, 2:3], 1, None, OP.add))
                    V.op(lambda e: e.tensor_scalar(
                        s_ii[:, 2:3], s_ii[:, 2:3], IBIG, None, OP.mult))
                    o10 = V.op(lambda e, ti=ti, kk=kk: e.tensor_tensor(
                        s_ids[:, kk * NT + ti:kk * NT + ti + 1], s_ii[:, 3:4],
                        s_ii[:, 2:3], OP.add))
                    pcol = 4 * ti + 2 * kk
                    scat_toks.append(G.op(
                        lambda e, ti=ti, kk=kk, pcol=pcol: e.indirect_dma_start(
                            out=tab_in,
                            out_offset=IndirectOffsetOnAxis(
                                s_ids[:, kk * NT + ti:kk * NT + ti + 1], 0),
                            in_=s_pair[:, pcol:pcol + 2], in_offset=None,
                            bounds_check=_bc(e, NEXP * CAP - 1), oob_is_err=False),
                        deps=[o10, ZF], inc=16))
            tok_tab = coll("AllReduce", OP.add, [tab_in], [tab_ar], ctab,
                           deps=scat_toks)
            run_block(B2)
        P12.close()

        with ExitStack() as B3:
            s_cfc = [sbuf(B3, f"s_cfc{i}", [128, HDIM], BF16) for i in range(KC)]
            s_cpr = [sbuf(B3, f"s_cpr{i}", [128, D], BF16) for i in range(HKC)]
            s_tab = sbuf(B3, "s_tab", [128, 80], F32)
            s_gw = sbuf(B3, "s_gw", [128, 40], F32)
            s_gif = sbuf(B3, "s_gif", [128, 40], F32)
            s_gm = sbuf(B3, "s_gm", [128, 40], F32)
            s_gi = sbuf(B3, "s_gi", [128, 40], I32)
            s_gt = [sbuf(B3, f"s_gt{i}", [128, D], BF16) for i in range(2)]
            s_bch = [sbuf(B3, f"s_bch{i}", [128, SC], BF16) for i in range(KC)]
            s_h = [sbuf(B3, f"s_h{i}", [128, SC], BF16) for i in range(HKC)]
            s_eo = [sbuf(B3, f"s_eo{i}", [128, D], BF16) for i in range(8)]

            G.op(lambda e: e.indirect_dma_start(
                out=s_tab[:], out_offset=None, in_=tab_ar,
                in_offset=IndirectOffsetOnAxis(s_tbix[:, 0:1], 0),
                bounds_check=_bc(e, NEXP * CAP - 1), oob_is_err=False),
                deps=[tok_tab], inc=16)
            t_tb = G.flush()
            for k in range(KC):
                dma(G, s_cfc[k][:], cfc_b[k * 128:(k + 1) * 128, :],
                    deps=[tcb])
            f_cfc = G.flush()
            cfl = {k: f_cfc for k in range(KC)}
            for k2 in range(HKC):
                dma(Q, s_cpr[k2][:], cpr_b[k2 * 128:(k2 + 1) * 128, :],
                    deps=[tcb])
            f_cpl = Q.flush()
            cpl = {k2: f_cpl for k2 in range(HKC)}
            tgw = V.op(lambda e: e.tensor_copy(s_gw[:], s_tab[:, 1:80:2]),
                       deps=[t_tb])
            tgf = V.op(lambda e: e.tensor_copy(s_gif[:], s_tab[:, 0:80:2]),
                       deps=[t_tb])
            V.op(lambda e: e.tensor_scalar(
                s_gm[:], s_gif[:], 0.0, None, OP.is_equal), deps=[tgf])
            V.op(lambda e: e.tensor_scalar(
                s_gif[:], s_gif[:], -1.0, None, OP.add))
            V.op(lambda e: e.scalar_tensor_tensor(
                s_gif[:], s_gm[:], float(IBIG), s_gif[:], OP.mult, OP.add))
            tgi = V.op(lambda e: e.tensor_copy(s_gi[:], s_gif[:]))

            bank_free = {i: None for i in range(8)}
            gt_free = [None, None]
            eo_free = [None] * 8
            mm1_toks = {}
            mm2_prev = None
            rs_scat = {0: [], 1: []}
            tok_rs0 = None
            gts_all = {}

            def emit_gather(j):
                buf = j % 2
                tz = V.op(lambda e, buf=buf: e.memset(s_gt[buf][:], 0.0),
                          deps=[gt_free[buf]])
                tgt = G.op(lambda e, j=j, buf=buf: e.indirect_dma_start(
                    out=s_gt[buf][:], out_offset=None, in_=xf_ag,
                    in_offset=IndirectOffsetOnAxis(s_gi[:, j:j + 1], 0),
                    bounds_check=_bc(e, 8 * TOK - 1), oob_is_err=False),
                    deps=[tz, tgi, tok_xf], inc=0,
                    aux="ggt0" if j % 2 == 0 else "ggt1")
                tst = dma(Q, batch_d[j * 128:(j + 1) * 128, :],
                          s_gt[buf][:], deps=[tgt],
                          aux="qst0" if j % 2 == 0 else "qst1")
                gt_free[buf] = tst
                gts_all[j] = tst

            def emit_transposes(sc):
                for k in range(KC):
                    btok = S.op(
                        lambda e, sc=sc, k=k: e.dma_start_transpose(
                            s_bch[k][:],
                            batch_d[sc * SC:(sc + 1) * SC,
                                    k * 128:(k + 1) * 128]),
                        deps=[gts_all[sc * 4 + 2], gts_all[sc * 4 + 3],
                              mm1_toks.get(sc - 1)],
                        inc=0, aux="sbch0" if sc % 2 == 0 else "sbch1")
                bch_tok[sc] = btok

            bch_tok = {}
            for j in range(4 * NSC):
                emit_gather(j)
            emit_transposes(0)
            for sc in range(NSC):
                h_done = []
                for m2 in range(HKC):
                    bank = m2 % 6
                    for k in range(KC):
                        T.op(lambda e, m2=m2, k=k, bank=bank: e.matmul(
                            pb[bank][:, 0:SC],
                            s_cfc[k][:, m2 * 128:(m2 + 1) * 128],
                            s_bch[k][:],
                            start=(k == 0), stop=(k == KC - 1)),
                            deps=[bch_tok[sc], cfl[k]] +
                                 ([bank_free[bank]] if k == 0 else []))
                    hv = S.op(lambda e, m2=m2, bank=bank: e.activation(
                        s_h[m2][:], pb[bank][:, 0:SC], AF.Gelu),
                        deps=[T.tok(), mm2_prev])
                    bank_free[bank] = hv
                    h_done.append(hv)
                mm1_toks[sc] = T.tok()
                h_all = S.tok()
                for j4 in range(4):
                    j = sc * 4 + j4
                    ebuf = j % 8
                    for k2 in range(HKC):
                        for half in range(2):
                            T.op(lambda e, j4=j4, half=half, k2=k2:
                                 e.matmul(
                                     pb[6 + half][:, 0:512],
                                     s_h[k2][:, j4 * 128:(j4 + 1) * 128],
                                     s_cpr[k2][:, half * 512:(half + 1) * 512],
                                     start=(k2 == 0), stop=(k2 == HKC - 1)),
                                 deps=[h_all, cpl[k2]] +
                                      ([bank_free[6], bank_free[7]]
                                       if k2 == 0 and half == 0 else []))
                    mm2_tok = T.tok()
                    for half in range(2):
                        tev = V.op(lambda e, j=j, half=half, ebuf=ebuf:
                                   e.tensor_scalar_mul(
                                       s_eo[ebuf][:, half * 512:(half + 1) * 512],
                                       pb[6 + half][:, 0:512], s_gw[:, j:j + 1]),
                                   deps=[mm2_tok, eo_free[ebuf], tgw])
                        bank_free[6 + half] = tev
                    tsc = G.op(lambda e, j=j, ebuf=ebuf:
                               e.indirect_dma_start(
                                   out=rs_in,
                                   out_offset=IndirectOffsetOnAxis(
                                       s_gi[:, j:j + 1], 0),
                                   in_=s_eo[ebuf][:], in_offset=None,
                                   bounds_check=_bc(e, 8 * TOK - 1),
                                   oob_is_err=False),
                               deps=[V.tok(), ZF], inc=0,
                               aux=f"geo{j % 8}")
                    eo_free[ebuf] = tsc
                    rs_scat[0].append(tsc)
                mm2_prev = T.tok()
                if sc + 1 < NSC:
                    emit_transposes(sc + 1)
            tok_rs = coll("ReduceScatter", OP.add, [rs_in], [rs_out],
                          crs0, deps=rs_scat[0])
            run_block(B3)

        with ExitStack() as B4:
            s_f0 = [sbuf(B4, f"s_f0{i}", [128, D], BF16) for i in range(4)]
            s_x1c = [sbuf(B4, f"s_x1c{i}", [128, D], F32) for i in range(NT)]
            s_xr = [sbuf(B4, f"s_xr{i}", [128, D], F32) for i in range(4)]
            s_o = [sbuf(B4, f"s_o{i}", [128, D], F32) for i in range(2)]
            s_c4 = sbuf(B4, "s_c4", [128, 4 * NT], F32)
            s_sq4 = sbuf(B4, "s_sq4", [128, 512], F32)
            cons_tok = {}
            out_tok = {}
            stage = {}
            for ti in range(NT):
                lxt = dma(Q, s_x1c[ti][:], x1_sp[ti * 128:(ti + 1) * 128, :],
                          deps=[f_xf], aux="qst0")
            lx_all = Q.flush()

            def finB1(ti):
                a = 4 * ti
                q2 = stage[ti]
                tva = V.op(lambda e, a=a: e.tensor_add(
                    s_c4[:, a + 2:a + 3], s_c4[:, a + 2:a + 3],
                    s_c4[:, a + 3:a + 4]), deps=[q2])
                tvb = V.op(lambda e, a=a: e.tensor_scalar_mul(
                    s_c4[:, a + 2:a + 3], s_c4[:, a + 2:a + 3], 1.0 / D),
                    deps=[tva])
                r1 = S.op(lambda e, a=a: e.activation(
                    s_c4[:, a + 3:a + 4], s_c4[:, a + 2:a + 3], AF.Ln,
                    bias=s_eps[:, 0:1]), deps=[tvb])
                S.op(lambda e, a=a: e.activation(
                    s_c4[:, a:a + 1], s_c4[:, a + 3:a + 4], AF.Exp,
                    scale=s_mhalf[:, 0:1]), deps=[r1])

            def finB2(ti):
                a = 4 * ti
                rdone = S.tok()
                to = V.op(lambda e, ti=ti, a=a: e.tensor_scalar(
                    s_o[ti % 2][:], s_xr[ti % 4][:], s_c4[:, a + 1:a + 2],
                    s_c4[:, a:a + 1], OP.add, OP.mult),
                    deps=[rdone, out_tok.get(ti - 2)])
                out_tok[ti] = dma(Q, dout[ti * 128:(ti + 1) * 128, :],
                                  s_o[ti % 2][:], deps=[to],
                                  aux="qso0" if ti % 2 == 0 else "qso1")

            for ti in range(NT):
                a = 4 * ti
                fx = "qf0" if ti % 2 == 0 else "qf1"
                l0 = dma(S, s_f0[ti % 4][:], rs_out[ti * 128:(ti + 1) * 128, :],
                         deps=[tok_rs, lx_all, cons_tok.get(ti - 4)],
                         aux="sxt0" if ti % 2 == 0 else "sxt1")
                tb = V.op(lambda e, ti=ti: e.tensor_add(
                    s_xr[ti % 4][:], s_f0[ti % 4][:], s_x1c[ti][:]),
                    deps=[l0])
                cons_tok[ti] = tb
                tsum = V.op(lambda e, a=a, ti=ti: e.reduce_sum(
                    s_c4[:, a:a + 1], s_xr[ti % 4][:], axis=AX.X), deps=[tb])
                V.op(lambda e, a=a: e.tensor_scalar_mul(
                    s_c4[:, a + 1:a + 2], s_c4[:, a:a + 1], -1.0 / D),
                    deps=[tsum])
                tmn = V.tok()
                q1 = S.op(lambda e, ti=ti, a=a: e.activation(
                    s_sq4[:], s_xr[ti % 4][:, 0:512], AF.Square,
                    bias=s_c4[:, a + 1:a + 2],
                    accum_out=s_c4[:, a + 2:a + 3]), deps=[tmn])
                stage[ti] = S.op(lambda e, ti=ti, a=a: e.activation(
                    s_sq4[:], s_xr[ti % 4][:, 512:1024], AF.Square,
                    bias=s_c4[:, a + 1:a + 2],
                    accum_out=s_c4[:, a + 3:a + 4]), deps=[q1])
                if ti >= 1:
                    finB1(ti - 1)
                if ti >= 2:
                    finB2(ti - 2)
            finB1(NT - 1)
            finB2(NT - 2)
            finB2(NT - 1)
            Q.wait([out_tok[NT - 1], out_tok[NT - 2]])
            run_block(B4)

    return nc


def _host_inputs(inputs, core):
    b, j = core // 2, core % 2
    x = np.asarray(inputs["L0_attn_x"])
    xb = x[b]
    lo = j * TOK
    bidx = (lo - 1) % N
    xch = np.ascontiguousarray(
        np.concatenate([xb.T[:, bidx:bidx + 1], xb.T[:, lo:lo + TOK]], axis=1))
    xtok = np.ascontiguousarray(xb[lo:lo + TOK])

    def vec128(v):
        return np.ascontiguousarray(np.asarray(v, np.float32).reshape(KC, 128).T)

    selc = np.zeros((8, 1), np.float32)
    if j == 1:
        selc[core - 1, 0] = 1.0
    selp = np.zeros((8, 2), np.float32)
    selp[:core, 0] = 1.0
    selp[:, 1] = 1.0
    tokid = (np.float32(core * TOK + 1)
             + np.arange(128, dtype=np.float32)[:, None]
             + 128.0 * np.arange(NT, dtype=np.float32)[None, :])
    tabix = (core * CAP + 40 * np.arange(128, dtype=np.int32))[:, None]

    return dict(
        xch=xch, xtok=xtok,
        wgT=np.ascontiguousarray(np.asarray(inputs["L0_attn_Wg_w"]).T),
        wkT=np.ascontiguousarray(np.asarray(inputs["L0_attn_Wk_w"]).T),
        wvT=np.ascontiguousarray(np.asarray(inputs["L0_attn_Wv_w"]).T),
        wrT=np.ascontiguousarray(np.asarray(inputs["L0_attn_Wr_w"]).T),
        woT=np.ascontiguousarray(np.asarray(inputs["L0_attn_Wo_w"]).T),
        won=np.ascontiguousarray(np.asarray(inputs["L0_attn_Wo_w"])),
        bg=vec128(inputs["L0_attn_Wg_b"]), bk=vec128(inputs["L0_attn_Wk_b"]),
        bv=vec128(inputs["L0_attn_Wv_b"]), br=vec128(inputs["L0_attn_Wr_b"]),
        wob=np.ascontiguousarray(np.asarray(inputs["L0_attn_Wo_b"],
                                            np.float32)).reshape(1, D),
        mg=vec128(inputs["L0_attn_mg"]), mk=vec128(inputs["L0_attn_mk"]),
        mv=vec128(inputs["L0_attn_mv"]), mr=vec128(inputs["L0_attn_mr"]),
        wdec=vec128(np.asarray(inputs["L0_attn_w"]).reshape(-1)),
        uu=vec128(np.asarray(inputs["L0_attn_u"]).reshape(-1)),
        wg=np.ascontiguousarray(np.asarray(inputs["L0_ffn_w_g"])),
        cfc=np.ascontiguousarray(np.asarray(inputs["L0_ffn_c_fc"])[core]),
        cpr=np.ascontiguousarray(np.asarray(inputs["L0_ffn_c_proj"])[core]),
        lt128=np.ascontiguousarray(np.tril(np.ones((128, 128), np.float32), -1)),
        lt16=np.ascontiguousarray(np.tril(np.ones((16, 16), np.float32), -1)),
        ones128=np.ones((128, 128), np.float32),
        id8=np.eye(8, dtype=np.float32),
        iota8=np.ascontiguousarray(np.tile(np.arange(8, dtype=np.float32),
                                           (128, 1))),
        iorev=np.ascontiguousarray(np.tile(
            np.arange(CH, dtype=np.float32) - (CH - 1), (128, 1))),
        indA=np.ascontiguousarray(
            np.repeat(np.eye(H, dtype=np.float32), 64, axis=0) / 64.0),
        indB=np.ascontiguousarray(
            np.repeat(np.eye(H, dtype=np.float32), 64, axis=1)),
        selc=selc, selp=selp, tokid=np.ascontiguousarray(tokid),
        wobc=vec128(inputs["L0_attn_Wo_b"]),
        tabix=np.ascontiguousarray(tabix.astype(np.int32)),
        mask8=np.tile(np.eye(8, dtype=np.float32), (16, 1)),
        sel16=np.repeat(np.eye(16, dtype=np.float32), 8, axis=0),
    )


_NC_CACHE = {}


def kernel(**inputs):
    if "nc" not in _NC_CACHE:
        _NC_CACHE["nc"] = build_nc()
    nc = _NC_CACHE["nc"]
    in_maps = [_host_inputs(inputs, c) for c in range(8)]
    res = run_bass_kernel_spmd(nc, in_maps, core_ids=list(range(8)))
    out = np.stack([np.asarray(res.results[c]["out"]) for c in range(8)], axis=0)
    return np.ascontiguousarray(out.reshape(B, N, D).astype(np.float32))


# version 31
# speedup vs baseline: 1.0439x; 1.0439x over previous
import numpy as np
import ml_dtypes
import concourse.bass as bass
import concourse.mybir as mybir
from concourse.bass_utils import run_bass_kernel_spmd
from concourse.bass_core_rust import IndirectOffsetOnAxis
from contextlib import ExitStack

F32 = mybir.dt.float32
BF16 = mybir.dt.bfloat16
I32 = mybir.dt.int32
AF = mybir.ActivationFunctionType
OP = mybir.AluOpType
AX = mybir.AxisListType

B, N, D, H = 4, 4096, 1024, 16
TOK = 2048
CH = 512
NCH = TOK // CH
NT = TOK // 128
KC = D // 128
NEXP, CAP = 8, 5120
SC = 512
NSC = CAP // SC
HDIM = 4096
HKC = HDIM // 128
EPS = 1e-5
IBIG = 1 << 24

DBG = False
DBG2 = False


class Rec:

    def __init__(self, name, self_serialize):
        self.name, self.self_ser = name, self_serialize
        self.ops, self.n, self.sem = [], 0, None
        self.aux = {}
        self.flag = None
        self.flag_n = 0

    def op(self, fn, deps=(), inc=1, aux=None):
        if aux is not None:
            s = self.aux[aux]
            def wrap(e, fn=fn, s=s):
                inst = fn(e)
                inst.then_inc(s[0], 16)
                return inst
            self.ops.append((tuple(d for d in deps if d is not None), wrap, 0))
            s[1] += 16
            return (s[0], s[1])
        self.ops.append((tuple(d for d in deps if d is not None), fn, inc))
        self.n += inc
        return (self, self.n)

    def wait(self, deps):
        self.ops.append((tuple(d for d in deps if d is not None), None, 0))

    def flush(self):
        deps = [(self, self.n)]
        for s in self.aux.values():
            deps.append((s[0], s[1]))
        self.flag_n += 1
        self.ops.append((tuple(deps),
                         lambda e, f=self.flag: e.sem_inc(f, 1), 0))
        return (self.flag, self.flag_n)

    def tok(self):
        return (self, self.n)

    def take(self):
        ops, self.ops = self.ops, []
        return ops


def _noinst(x):
    return x


_BCREG = {}


def _bc(e, val):
    key = (id(e), val)
    if key not in _BCREG or _BCREG[key][0] is not e:
        _BCREG[key] = (e, e.to_reg(val))
    return _BCREG[key][1]


def _replay(eng, rec, ops, base):
    cnt = base
    for deps, fn, inc in ops:
        for dsem, dval in deps:
            sem = dsem.sem if isinstance(dsem, Rec) else dsem
            eng.wait_ge(sem, dval)
        if fn is None:
            continue
        if rec.self_ser and cnt:
            eng.wait_ge(rec.sem, cnt)
        inst = fn(eng)
        if inc:
            inst.then_inc(rec.sem, inc)
        cnt += inc
    return cnt


def build_nc():
    nc = bass.Bass("TRN2", target_bir_lowering=False, debug=False)

    def din(name, shape, dt=F32):
        return nc.dram_tensor(name, shape, dt, kind="ExternalInput").ap()

    dxch = din("xch", [D, TOK + 1])
    dxtok = din("xtok", [TOK, D])
    dwT = {p: din(f"w{p}T", [D, D]) for p in "gkvr"}
    dwoT = din("woT", [D, D])
    dwon = din("won", [D, D])
    dbias = {p: din(f"b{p}", [128, KC]) for p in "gkvr"}
    dwob = din("wob", [1, D])
    dmixv = {p: din(f"m{p}", [128, KC]) for p in "gkvr"}
    dwdec = din("wdec", [128, KC])
    du = din("uu", [128, KC])
    dwg = din("wg", [D, NEXP])
    dcfc = din("cfc", [D, HDIM])
    dcpr = din("cpr", [HDIM, D])
    dlt128 = din("lt128", [128, 128])
    dlt16 = din("lt16", [16, 16])
    dones = din("ones128", [128, 128])
    did8 = din("id8", [8, 8])
    diota8 = din("iota8", [128, NEXP])
    diorev = din("iorev", [128, CH])
    dindA = din("indA", [D, H])
    dindB = din("indB", [H, D])
    dselc = din("selc", [8, 1])
    dselp = din("selp", [8, 2])
    dtokid = din("tokid", [128, NT])
    dwobc = din("wobc", [128, KC])
    dtabix = din("tabix", [128, 1], I32)
    dmask8 = din("mask8", [128, 8])
    dsel16 = din("sel16", [128, 16])

    dout = nc.dram_tensor("out", [TOK, D], F32, kind="ExternalOutput").ap()
    if DBG:
        dx1d = nc.dram_tensor("x1d", [TOK, D], F32, kind="ExternalOutput").ap()
    if DBG2:
        dkvd = nc.dram_tensor("kvd", [KC * 128, TOK], F32, kind="ExternalOutput").ap()
        daccd = nc.dram_tensor("accd", [KC * 128, TOK], F32, kind="ExternalOutput").ap()
        dxmk = nc.dram_tensor("xmk", [KC * 128, TOK], BF16, kind="ExternalOutput").ap()
        dxmv = nc.dram_tensor("xmv", [KC * 128, TOK], BF16, kind="ExternalOutput").ap()
        dkev = nc.dram_tensor("kev", [KC * 128, TOK], F32, kind="ExternalOutput").ap()

    kv_sp = nc.dram_tensor("kv_sp", [KC, 128, TOK], F32).ap()
    sg_sp = nc.dram_tensor("sg_sp", [KC, 128, TOK], BF16).ap()
    rb_sp = nc.dram_tensor("rb_sp", [KC, 128, TOK], BF16).ap()
    x1_sp = nc.dram_tensor("x1_sp", [TOK, D], F32).ap()
    carry_in = nc.dram_tensor("carry_in", [KC, 128], F32).ap()
    carry_ag = nc.dram_tensor("carry_ag", [8, KC * 128], F32).ap()
    cnt_in = nc.dram_tensor("cnt_in", [1, 16], F32).ap()
    cnt_ag = nc.dram_tensor("cnt_ag", [8, 16], F32).ap()
    tot_b1 = nc.dram_tensor("tot_b1", [16, 8], F32).ap()
    tot_b2 = nc.dram_tensor("tot_b2", [16, 8], F32).ap()
    toff_b1 = nc.dram_tensor("toff_b1", [16, 8], F32).ap()
    toff_b2 = nc.dram_tensor("toff_b2", [16, 8], F32).ap()
    tab_in = nc.dram_tensor("tab_in", [NEXP * CAP, 2], F32).ap()
    tab_ar = nc.dram_tensor("tab_ar", [NEXP * CAP, 2], F32).ap()
    xf_in = nc.dram_tensor("xf_in", [TOK, D], BF16).ap()
    xf_ag = nc.dram_tensor("xf_ag", [8 * TOK, D], BF16).ap()
    xf_subH = [nc.dram_tensor(f"xf_subH{q}", [8 * 2 * CH, D], BF16).ap()
               for q in range(2)]
    batch_d = nc.dram_tensor("batch_d", [CAP, D], BF16).ap()
    rs_in = nc.dram_tensor("rs_in", [8 * TOK, D], BF16).ap()
    rs_out = nc.dram_tensor("rs_out", [TOK, D], BF16).ap()

    V = Rec("vector", True)
    S = Rec("scalar", True)
    T = Rec("tensor", False)
    Q = Rec("sync", False)
    G = Rec("gpsimd", False)
    RECS = (V, S, T, Q, G)
    bases = {r.name: 0 for r in RECS}

    with ExitStack() as top:
        for r in RECS:
            r.sem = top.enter_context(nc.semaphore(f"sem_{r.name}"))
        cag = top.enter_context(nc.semaphore("cag"))
        ccnt = top.enter_context(nc.semaphore("ccnt"))
        ctab = top.enter_context(nc.semaphore("ctab"))
        cxf = top.enter_context(nc.semaphore("cxf"))
        cxfs = [cxf] + [top.enter_context(nc.semaphore(f"cxf{q}"))
                        for q in range(1, NCH)]
        crs0 = top.enter_context(nc.semaphore("crs0"))
        crs1 = top.enter_context(nc.semaphore("crs1"))
        Q.flag = top.enter_context(nc.semaphore("qflag"))
        G.flag = top.enter_context(nc.semaphore("gflag"))
        gcomp = top.enter_context(nc.semaphore("gcomp"))
        gcnt = [0]

        def pop(fn, deps=()):
            mydeps = tuple(d for d in deps if d is not None)
            prev = gcnt[0]
            gcnt[0] += 1
            def wrap(e, fn=fn, prev=prev):
                if prev:
                    e.wait_ge(gcomp, prev)
                inst = fn(e)
                inst.then_inc(gcomp, 1)
                return inst
            G.ops.append((mydeps, wrap, 0))
            return (gcomp, gcnt[0])
        for nm in ("qx", "qds0", "qds1", "qxt0", "qxt1", "qm32",
                   "qxf0", "qxf1",
                   "qbch0", "qbch1", "qst0", "qst1", "qf0", "qf1",
                   "qxp0", "qxp1", "qso0", "qso1"):
            Q.aux[nm] = [top.enter_context(nc.semaphore(nm)), 0]
        for nm in ("sbch0", "sbch1", "sxt0", "sxt1"):
            S.aux[nm] = [top.enter_context(nc.semaphore(nm)), 0]
        for nm in ("ggt0", "ggt1", "geo0", "geo1", "geo2", "geo3",
                   "geo4", "geo5", "geo6", "geo7", "gxfc"):
            G.aux[nm] = [top.enter_context(nc.semaphore(nm)), 0]
        Q.aux["qm32b"] = [top.enter_context(nc.semaphore("qm32b")), 0]

        def sbuf(stk, name, shape, dt):
            return stk.enter_context(nc.sbuf_tensor(name, shape, dt))

        pb = [top.enter_context(nc.psum_tensor(f"pb{i}", [128, 512], F32))
              for i in range(8)]

        def dma(rec, out, in_, deps=(), aux=None):
            return rec.op(lambda e, o=out, i=in_: e.dma_start(out=o, in_=i),
                          deps=deps, inc=16, aux=aux)

        def coll(kind, op, ins, outs, sem, deps=()):
            def f(e, kind=kind, op=op, ins=ins, outs=outs, sem=sem):
                return e.collective_compute(
                    kind, op, replica_groups=[list(range(8))],
                    ins=[a.opt() for a in ins],
                    outs=[a.opt() for a in outs]).then_inc(sem, 1)
            G.ops.append((tuple(d for d in deps if d is not None), f, 0))
            return (sem, 1)

        def run_block(stk):
            blk = stk.enter_context(nc.Block())
            for r in RECS:
                ops = r.take()
                if not ops:
                    continue
                def body(eng, r=r, ops=ops, base=bases[r.name]):
                    _replay(eng, r, ops, base)
                getattr(blk, r.name)(body)
                bases[r.name] = r.n

        s_eps = sbuf(top, "s_eps", [128, 1], F32)
        s_mhalf = sbuf(top, "s_mhalf", [128, 1], F32)
        s_col = sbuf(top, "s_col", [128, 8], F32)
        s_tbix = sbuf(top, "s_tbix", [128, 1], I32)
        P12 = top.enter_context(ExitStack())
        P = P12
        s_bias = {p: sbuf(P, f"s_b{p}", [128, KC], F32) for p in "gkvr"}
        s_mix = {p: sbuf(P, f"s_m{p}", [128, KC], F32) for p in "gkvr"}
        s_om = {p: sbuf(P, f"s_om{p}", [128, KC], F32) for p in "gkvr"}
        s_dec = sbuf(P, "s_dec", [128, KC], F32)
        s_e = sbuf(P, "s_e", [128, KC], F32)
        s_u = sbuf(P, "s_u", [128, KC], F32)
        s_d512 = sbuf(P, "s_d512", [128, KC * 3], F32)
        s_lt128 = sbuf(P, "s_lt128", [128, 128], F32)
        s_lt16 = sbuf(P, "s_lt16", [16, 16], F32)
        s_ones = sbuf(P, "s_ones", [128, 128], F32)
        s_id8 = sbuf(P, "s_id8", [8, 8], F32)
        s_iota8 = sbuf(P, "s_iota8", [128, NEXP], F32)
        s_iorev = sbuf(P, "s_iorev", [128, CH], F32)
        s_selc = sbuf(P, "s_selc", [8, 1], F32)
        s_selp = sbuf(P, "s_selp", [8, 2], F32)
        s_wg = sbuf(P, "s_wg", [128, KC * NEXP], F32)
        s_indA = sbuf(P, "s_indA", [128, KC * H], F32)
        s_indB = sbuf(P, "s_indB", [16, D], F32)
        s_neg1 = sbuf(P, "s_neg1", [128, 1], F32)
        s_n512 = sbuf(P, "s_n512", [128, 1], F32)
        s_wobc = sbuf(P, "s_wobc", [128, KC], F32)
        s_tcol = sbuf(P, "s_tcol", [128, 8], F32)
        s_wobr = sbuf(P, "s_wobr", [1, D], F32)
        s_carry = sbuf(P, "s_carry", [128, KC], F32)
        s_seed = sbuf(P, "s_seed", [128, KC], F32)
        s_lastc = sbuf(P, "s_lastc", [128, KC], F32)
        s_M32 = [sbuf(P, f"s_M32{i}", [128, NEXP], F32) for i in range(KC)]
        s_bb = sbuf(P, "s_bb", [8, 1], F32)
        s_csb = sbuf(P, "s_csb", [128, NEXP], F32)
        s_R1 = sbuf(P, "s_R1", [8, TOK], F32)
        s_lg = sbuf(P, "s_lg", [8, TOK], F32)
        s_mn = sbuf(P, "s_mn", [128, NT], F32)
        s_rstd = sbuf(P, "s_rstd", [128, NT], F32)
        s_oh = sbuf(P, "s_oh", [128, 128], F32)
        s_oh2 = sbuf(P, "s_oh2", [128, 128], F32)
        s_rk1 = sbuf(P, "s_rk1", [128, 128], F32)
        s_rk2 = sbuf(P, "s_rk2", [128, 128], F32)
        s_rt = sbuf(P, "s_rt", [128, 16], F32)
        s_tt16 = sbuf(P, "s_tt16", [16, 8], F32)
        s_tt16b = sbuf(P, "s_tt16b", [16, 8], F32)
        s_cnt8 = sbuf(P, "s_cnt8", [8, 2], F32)
        s_cnta = sbuf(P, "s_cnta", [8, 16], F32)
        s_offr = sbuf(P, "s_offr", [1, 16], F32)
        s_off = sbuf(P, "s_off", [128, 16], F32)
        s_tokid = sbuf(P, "s_tokid", [128, NT], F32)
        s_ids = sbuf(P, "s_ids", [128, 2 * NT], I32)
        s_idf = sbuf(P, "s_idf", [128, 3 * NT], F32)
        s_pair = sbuf(P, "s_pair", [128, 4 * NT], F32)
        s_ii = sbuf(P, "s_ii", [128, 8], I32)
        s_agc = sbuf(P, "s_agc", [8, KC * 128], F32)
        s_trow = sbuf(P, "s_trow", [1, 128], F32)
        s_trow2 = sbuf(P, "s_trow2", [1, 128], F32)
        s_mask8 = sbuf(P, "s_mask8", [128, 8], F32)
        s_sel16 = sbuf(P, "s_sel16", [128, 16], F32)
        s_m8s = sbuf(P, "s_m8s", [128, 8], F32)

        with ExitStack() as B1:
            s_wb = [sbuf(B1, f"s_wb{i}", [128, 4 * D], BF16) for i in range(KC)]
            s_x = [sbuf(B1, f"s_x{i}", [128, CH + 1], F32) for i in range(KC)]
            s_xma = [sbuf(B1, f"s_xma{i}", [128, CH], BF16) for i in range(KC)]
            s_xmb = [sbuf(B1, f"s_xmb{i}", [128, CH], BF16) for i in range(KC)]
            s_xmc = [sbuf(B1, f"s_xmc{i}", [128, CH], BF16) for i in range(KC)]
            s_xmd = [sbuf(B1, f"s_xmd{i}", [128, CH], BF16) for i in range(KC)]
            s_kv = [sbuf(B1, f"s_kv{i}", [128, CH], F32) for i in range(KC)]
            s_sg = [sbuf(B1, f"s_sg{i}", [128, CH], BF16) for i in range(KC)]
            s_rb = [sbuf(B1, f"s_rb{i}", [128, CH], BF16) for i in range(KC)]
            s_kscr = sbuf(B1, "s_kscr", [128, CH], F32)
            s_kscr2 = sbuf(B1, "s_kscr2", [128, CH], F32)
            s_ms = [sbuf(B1, f"s_ms{i}", [128, CH], F32) for i in range(4)]
            s_wrev = sbuf(B1, "s_wrev", [128, CH], F32)
            s_wrev2 = sbuf(B1, "s_wrev2", [128, CH], F32)
            s_cpart = sbuf(B1, "s_cpart", [128, 1], F32)
            s_zero = sbuf(B1, "s_zero", [128, 2048], BF16)

            for k in range(KC):
                xlt0 = dma(Q, s_x[k][:],
                           dxch[k * 128:(k + 1) * 128, 0:CH + 1], aux="qx")
            for p in "gkvr":
                dma(Q, s_bias[p][:], dbias[p])
                dma(Q, s_mix[p][:], dmixv[p])
            dma(Q, s_dec[:], dwdec)
            dma(Q, s_u[:], du)
            dma(Q, s_lt128[:], dlt128)
            dma(Q, s_lt16[:], dlt16)
            dma(Q, s_ones[:], dones)
            dma(Q, s_id8[:], did8)
            dma(Q, s_iota8[:], diota8)
            dma(Q, s_iorev[:], diorev)
            dma(Q, s_selc[:], dselc)
            dma(Q, s_selp[:], dselp)
            dma(Q, s_tokid[:], dtokid)
            dma(Q, s_mask8[:], dmask8)
            dma(Q, s_sel16[:], dsel16)
            dma(Q, s_tbix[:], dtabix)
            for k in range(KC):
                dma(Q, s_wg[:, k * NEXP:(k + 1) * NEXP], dwg[k * 128:(k + 1) * 128, :])
                dma(Q, s_indA[:, k * H:(k + 1) * H], dindA[k * 128:(k + 1) * 128, :])
            dma(Q, s_indB[:], dindB)
            dma(Q, s_wobc[:], dwobc)
            t_wobr = dma(Q, s_wobr[:], dwob)
            ld0 = Q.flush()

            V.op(lambda e: e.memset(s_eps[:], EPS))
            V.op(lambda e: e.memset(s_mhalf[:], -0.5))
            V.op(lambda e: e.memset(s_neg1[:], -1.0))
            V.op(lambda e: e.memset(s_n512[:], -512.0))
            t_z = V.op(lambda e: e.memset(s_zero[:], 0.0))

            t_cz = V.op(lambda e: e.memset(s_carry[:], 0.0))
            for p in "gkvr":
                V.op(lambda e, p=p: e.tensor_scalar(
                    s_om[p][:], s_mix[p][:], -1.0, 1.0, OP.mult, OP.add),
                    deps=[ld0])
            t_consts = V.tok()
            t_e = S.op(lambda e: e.activation(s_e[:], s_dec[:], AF.Exp),
                       deps=[ld0, t_consts])
            t_d = S.op(lambda e: e.activation(s_dec[:], s_e[:], AF.Exp,
                                              scale=s_neg1[:, 0:1]), deps=[t_e])
            t_d512 = S.op(lambda e: e.activation(s_d512[:, 0:KC], s_e[:], AF.Exp,
                                                 scale=s_n512[:, 0:1]), deps=[t_d])
            tprev = t_d512
            for kp in range(1, 3):
                tprev = V.op(lambda e, kp=kp: e.tensor_mul(
                    s_d512[:, kp * KC:(kp + 1) * KC],
                    s_d512[:, (kp - 1) * KC:kp * KC], s_d512[:, 0:KC]),
                    deps=[tprev])
            t_d512a = tprev
            wrev_free = {0: None, 1: None}

            for ko in range(KC):
                T.op(lambda e, ko=ko: e.matmul(
                    pb[5][0:NEXP, 0:1], s_wg[:, ko * NEXP:(ko + 1) * NEXP],
                    s_wobc[:, ko:ko + 1], start=(ko == 0), stop=(ko == KC - 1)),
                    deps=[ld0] if ko == 0 else ())
            t_bb = V.op(lambda e: e.tensor_copy(s_bb[:], pb[5][0:NEXP, 0:1]),
                        deps=[T.tok()])
            for ko in range(KC):
                T.op(lambda e, ko=ko: e.matmul(
                    pb[4][:, 0:NEXP], s_ones[:], s_wg[:, ko * NEXP:(ko + 1) * NEXP],
                    start=(ko == 0), stop=(ko == KC - 1)),
                    deps=[t_bb] if ko == 0 else ())
            t_csb = V.op(lambda e: e.tensor_copy(s_csb[:], pb[4][:, 0:NEXP]),
                         deps=[T.tok()])

            wld = {}
            for wi, p in enumerate("kvgr"):
                for k in range(KC):
                    dma(G, s_wb[k][:, wi * D:(wi + 1) * D],
                        dwT[p][k * 128:(k + 1) * 128, :])
                f_w = G.flush()
                for k in range(KC):
                    wld[(p, k)] = f_w
            for i in range(64):
                dma(G, rs_in[i * 256:(i + 1) * 256, :]
                    .rearrange("(p a) f -> p (a f)", p=128),
                    s_zero[:, 0:2048], deps=[t_z])
            dma(G, tab_in.rearrange("(p a) c -> p (a c)", p=128),
                s_zero[:, 0:640], deps=[t_z])
            ZF = G.tok()

            x_free = {k: None for k in range(KC)}
            spill_tok = {}
            f_sp = {}
            f_spb = {}
            bank_free = {b: None for b in range(6)}
            r1_free = t_csb
            scratch_free = {0: None, 1: None, 2: None, 3: None}
            kscr_free = {0: None, 1: None}
            kmm_prev = None
            grmm_done = None
            mixdump = {}

            def emit_mix(p, dst, k, xl, extra=()):
                si = k % 4
                scr = s_ms[si]
                t1 = S.op(lambda e, k=k, p=p, scr=scr: e.activation(
                    scr[:], s_x[k][:, 0:CH], AF.Identity,
                    scale=s_om[p][:, k:k + 1]),
                    deps=[xl[k], t_consts, scratch_free[si]] + list(extra))
                t2 = V.op(lambda e, k=k, p=p, dst=dst, scr=scr:
                          e.scalar_tensor_tensor(
                              dst[k][:], s_x[k][:, 1:CH + 1],
                              s_mix[p][:, k:k + 1], scr[:],
                              OP.mult, OP.add), deps=[t1])
                scratch_free[si] = t2
                return t2

            def load_x(cn, dep):
                for k in range(KC):
                    xlt = dma(Q, s_x[k][:],
                              dxch[k * 128:(k + 1) * 128,
                                   cn * CH:cn * CH + CH + 1],
                              deps=list(dep), aux="qx")
                return xlt
            xl_cur = xlt0
            for c in range(NCH):
                xl = {k: xl_cur for k in range(KC)}
                for k in range(KC):
                    T.op(lambda e, k=k: e.matmul(
                        pb[7][0:NEXP, 0:CH], s_wg[:, k * NEXP:(k + 1) * NEXP],
                        s_x[k][:, 1:CH + 1], start=(k == 0), stop=(k == KC - 1)),
                        deps=[xl[k]] + ([r1_free] if k == 0 else []))
                r1_free = V.op(lambda e, c=c: e.tensor_copy(
                    s_R1[:, c * CH:(c + 1) * CH], pb[7][0:NEXP, 0:CH]),
                    deps=[T.tok()])

                mixk = [emit_mix("k", s_xma, k, xl, extra=[kmm_prev])
                        for k in range(KC)]
                mixv = [emit_mix("v", s_xmb, k, xl, extra=[kmm_prev])
                        for k in range(KC)]
                mix_done = V.tok()
                xl_cur = load_x((c + 1) % NCH, [mix_done, r1_free])

                ev_kv = {}
                for m in range(KC):
                    bk, bv = m % 3, 3 + m % 3
                    for k in range(KC):
                        T.op(lambda e, k=k, m=m, bk=bk: e.matmul(
                            pb[bk][:, 0:CH],
                            s_wb[k][:, m * 128:(m + 1) * 128],
                            s_xma[k][:], start=(k == 0), stop=(k == KC - 1)),
                            deps=[mixk[k], wld[("k", k)]] +
                                 ([bank_free[bk]] if k == 0 else []))
                    kscr = (s_kscr, s_kscr2)[m % 2]
                    t_k = S.op(lambda e, m=m, bk=bk, kscr=kscr: e.activation(
                        kscr[:], pb[bk][:, 0:CH], AF.Identity,
                        bias=s_bias["k"][:, m:m + 1]),
                        deps=[T.tok(), kscr_free[m % 2]])
                    bank_free[bk] = t_k
                    for k in range(KC):
                        T.op(lambda e, k=k, m=m, bv=bv: e.matmul(
                            pb[bv][:, 0:CH],
                            s_wb[k][:, D + m * 128:D + (m + 1) * 128],
                            s_xmb[k][:], start=(k == 0), stop=(k == KC - 1)),
                            deps=[mixv[k], wld[("v", k)]] +
                                 ([bank_free[bv]] if k == 0 else []))
                    t_kv = V.op(lambda e, m=m, bv=bv, kscr=kscr:
                                e.scalar_tensor_tensor(
                        s_kv[m][:], pb[bv][:, 0:CH], s_bias["v"][:, m:m + 1],
                        kscr[:], OP.add, OP.mult),
                        deps=[T.tok(), t_k, f_sp.get(c - 1)])
                    bank_free[bv] = t_kv
                    kscr_free[m % 2] = t_kv
                    ev_kv[m] = t_kv
                kmm_prev = T.tok()

                for k in range(KC):
                    spill_tok[(c, k, "kv")] = dma(
                        Q, kv_sp[k, :, c * CH:(c + 1) * CH], s_kv[k][:],
                        deps=[ev_kv[k]])
                    if DBG2:
                        spill_tok[(c, k, "kv")] = dma(
                            G, dkvd[k * 128:(k + 1) * 128, c * CH:(c + 1) * CH],
                            s_kv[k][:], deps=[ev_kv[k]])
                    wrv = (s_wrev, s_wrev2)[k % 2]
                    kscr_d = (s_kscr, s_kscr2)[k % 2]
                    tw = S.op(lambda e, k=k, wrv=wrv: e.activation(
                        wrv[:], s_iorev[:], AF.Exp, scale=s_e[:, k:k + 1]),
                        deps=[t_d512a, wrev_free[k % 2]])
                    tp = V.op(lambda e, k=k, wrv=wrv, kscr_d=kscr_d:
                              e.scalar_tensor_tensor(
                        kscr_d[:], wrv[:], 1.0,
                        s_kv[k][:], OP.mult, OP.mult,
                        accum_out=s_cpart[:, 0:1]),
                        deps=[tw, ev_kv[k], t_cz])
                    wrev_free[k % 2] = tp
                    kscr_free[k % 2] = tp
                    if c < NCH - 1:
                        pw = (NCH - 2 - c) * KC + k
                        V.op(lambda e, pw=pw, k=k: e.scalar_tensor_tensor(
                            s_carry[:, k:k + 1], s_cpart[:, 0:1],
                            s_d512[:, pw:pw + 1], s_carry[:, k:k + 1],
                            OP.mult, OP.add), deps=[tp])
                    else:
                        V.op(lambda e, k=k: e.tensor_add(
                            s_carry[:, k:k + 1], s_carry[:, k:k + 1],
                            s_cpart[:, 0:1]), deps=[tp])
                    x_free[k] = mixv[k]
                f_sp[c] = Q.flush()

            t_call = V.tok()
            for k in range(KC):
                dma(Q, carry_in[k:k + 1, :], s_carry[:, k:k + 1], deps=[t_call])
            f_car = Q.flush()
            tok_cag = coll("AllGather", OP.bypass, [carry_in], [carry_ag], cag,
                           deps=[f_car, ZF])

            for c in range(NCH):
                xl = {k: xl_cur for k in range(KC)}
                mixg = [emit_mix("g", s_xmc, k, xl, extra=[grmm_done])
                        for k in range(KC)]
                mixr = [emit_mix("r", s_xmd, k, xl, extra=[grmm_done])
                        for k in range(KC)]
                mixb_done = V.tok()
                if c + 1 < NCH:
                    xl_cur = load_x(c + 1, [mixb_done])
                ev_g, ev_r = {}, {}
                for m in range(KC):
                    bg, br = m % 3, 3 + m % 3
                    for k in range(KC):
                        T.op(lambda e, k=k, m=m, bg=bg: e.matmul(
                            pb[bg][:, 0:CH],
                            s_wb[k][:, 2 * D + m * 128:2 * D + (m + 1) * 128],
                            s_xmc[k][:], start=(k == 0), stop=(k == KC - 1)),
                            deps=[mixg[k], wld[("g", k)]] +
                                 ([bank_free[bg]] if k == 0 else []))
                    ev_g[m] = S.op(lambda e, m=m, bg=bg: e.activation(
                        s_sg[m][:], pb[bg][:, 0:CH], AF.Sigmoid,
                        bias=s_bias["g"][:, m:m + 1]),
                        deps=[T.tok(), f_spb.get(c - 1)])
                    bank_free[bg] = ev_g[m]
                    for k in range(KC):
                        T.op(lambda e, k=k, m=m, br=br: e.matmul(
                            pb[br][:, 0:CH],
                            s_wb[k][:, 3 * D + m * 128:3 * D + (m + 1) * 128],
                            s_xmd[k][:], start=(k == 0), stop=(k == KC - 1)),
                            deps=[mixr[k], wld[("r", k)]] +
                                 ([bank_free[br]] if k == 0 else []))
                    ev_r[m] = S.op(lambda e, m=m, br=br: e.activation(
                        s_rb[m][:], pb[br][:, 0:CH], AF.Identity,
                        bias=s_bias["r"][:, m:m + 1]),
                        deps=[T.tok(), f_spb.get(c - 1)])
                    bank_free[br] = ev_r[m]
                grmm_done = T.tok()
                for k in range(KC):
                    dma(Q, sg_sp[k, :, c * CH:(c + 1) * CH], s_sg[k][:],
                        deps=[ev_g[k]])
                    dma(Q, rb_sp[k, :, c * CH:(c + 1) * CH], s_rb[k][:],
                        deps=[ev_r[k]])
                    x_free[k] = mixr[k]
                f_spb[c] = Q.flush()

            f_b1 = Q.flush()
            run_block(B1)

        with ExitStack() as B2:
            s_kvA = [sbuf(B2, f"s_kvA{i}", [128, CH], F32) for i in range(KC)]
            s_kvB = [sbuf(B2, f"s_kvB{i}", [128, CH], F32) for i in range(KC)]
            s_sgA = [sbuf(B2, f"s_sgA{i}", [128, CH], BF16) for i in range(KC)]
            s_sgB = [sbuf(B2, f"s_sgB{i}", [128, CH], BF16) for i in range(KC)]
            s_rbA = [sbuf(B2, f"s_rbA{i}", [128, CH], BF16) for i in range(KC)]
            s_rbB = [sbuf(B2, f"s_rbB{i}", [128, CH], BF16) for i in range(KC)]
            s_acc = [sbuf(B2, f"s_acc{i}", [128, CH], F32) for i in range(KC)]
            s_wk = [sbuf(B2, f"s_wk{i}", [128, CH], F32) for i in range(KC)]
            s_opb = [sbuf(B2, f"s_opb{i}", [128, CH], BF16) for i in range(KC)]
            s_opf = [sbuf(B2, f"s_opf{i}", [128, CH], F32) for i in range(KC)]
            s_woT = [sbuf(B2, f"s_woT{i}", [128, D], BF16) for i in range(KC)]
            s_won = [sbuf(B2, f"s_won{i}", [128, 128], F32) for i in range(2)]
            s_sq = sbuf(B2, "s_sq", [128, CH], F32)
            s_l1 = sbuf(B2, "s_l1", [128, CH], F32)
            s_l2 = sbuf(B2, "s_l2", [128, CH], F32)
            s_hl = sbuf(B2, "s_hl", [16, 2 * CH], F32)
            s_xt = [sbuf(B2, f"s_xt{i}", [128, D], F32) for i in range(2)]
            s_xr2 = [sbuf(B2, f"s_xr2{i}", [128, D], F32) for i in range(2)]
            s_x1 = sbuf(B2, "s_x1t", [128, D], F32)
            s_x1b = sbuf(B2, "s_x1bt", [128, D], BF16)

            dma(Q, s_agc[:], carry_ag, deps=[tok_cag])
            t_agl = Q.flush()
            for k in range(KC):
                T.op(lambda e, k=k: e.matmul(
                    pb[0][:, k:k + 1], s_agc[0:8, k * 128:(k + 1) * 128],
                    s_selc[:], start=True, stop=True), deps=[t_agl])
            t_seed = V.op(lambda e: e.tensor_copy(s_seed[:], pb[0][:, 0:KC]),
                          deps=[T.tok()])

            t_m32 = []
            mm_toks = [t_agl, t_agl]
            ii = 0
            for mc in range(KC):
                for ko in range(KC):
                    par = ii % 2
                    lw = dma(Q, s_won[par][:],
                             dwon[ko * 128:(ko + 1) * 128, mc * 128:(mc + 1) * 128],
                             deps=[mm_toks[par]], aux="qm32" if par == 0 else "qm32b")
                    T.op(lambda e, ko=ko, par=par: e.matmul(
                        pb[1][0:128, 0:NEXP], s_won[par][:],
                        s_wg[:, ko * NEXP:(ko + 1) * NEXP],
                        start=(ko == 0), stop=(ko == KC - 1)),
                        deps=[lw] + ([t_m32[-1]] if ko == 0 and t_m32 else []))
                    mm_toks[par] = T.tok()
                    ii += 1
                t_m32.append(V.op(lambda e, mc=mc: e.tensor_copy(
                    s_M32[mc][:], pb[1][:, 0:NEXP]), deps=[T.tok()]))

            for k in range(KC):
                dma(G, s_woT[k][:], dwoT[k * 128:(k + 1) * 128, :])
            f_wot = G.flush()
            wotl = {k: f_wot for k in range(KC)}

            SETS = {0: (s_kvA, s_sgA, s_rbA), 1: (s_kvB, s_sgB, s_rbB)}
            pre = {}
            for k in range(KC):
                dma(Q, s_kvA[k][:], kv_sp[k, :, 0:CH], deps=[f_b1], aux="qds0")
                dma(Q, s_sgA[k][:], sg_sp[k, :, 0:CH], deps=[f_b1], aux="qds0")
                t0 = dma(Q, s_rbA[k][:], rb_sp[k, :, 0:CH], deps=[f_b1],
                         aux="qds0")
            for k in range(KC):
                for w in ("kv", "sg", "rb"):
                    pre[(0, k, w)] = t0

            bank_free = {i: None for i in range(8)}
            bank_free[0] = t_seed
            bank_free[1] = t_m32[-1]
            accdumps = {}
            lg_done = {}
            bank7_free = [None]
            t_rw_all = {}
            guard_T = {}
            guard_S = {}
            prev_chunk_V = None
            prev_chunk_T = None
            x1_exports = []
            x1sp_tok = {}
            xfe_tok = {}
            xt_consumed = {}
            xp_tok = {}

            def emit_router_tile(ti):
                T.op(lambda e, ti=ti: e.transpose(
                    pb[7][:, 0:8], s_lg[0:8, ti * 128:(ti + 1) * 128], s_id8[:]),
                    deps=[lg_done[ti], bank7_free[0]])
                ttr = T.tok()
                ta = V.op(lambda e, ti=ti: e.scalar_tensor_tensor(
                    s_rt[:, 0:8], s_csb[:], s_mn[:, ti:ti + 1], pb[7][:, 0:8],
                    OP.mult, OP.add), deps=[ttr])
                bank7_free[0] = ta
                V.op(lambda e, ti=ti: e.tensor_scalar_mul(
                    s_rt[:, 0:8], s_rt[:, 0:8], s_rstd[:, ti:ti + 1]), deps=[ta])
                V.op(lambda e: e.reduce_max(s_rt[:, 8:9], s_rt[:, 0:8],
                                            axis=AX.X))
                V.op(lambda e: e.tensor_scalar(
                    s_col[:, 0:8], s_rt[:, 0:8], s_rt[:, 8:9], None,
                    OP.is_equal))
                V.op(lambda e: e.tensor_tensor_scan(
                    s_idf[:, 0:8], s_col[:, 0:8], s_col[:, 0:8], 0.0,
                    OP.add, OP.bypass))
                V.op(lambda e: e.tensor_scalar(
                    s_idf[:, 0:8], s_idf[:, 0:8], 1.0, None, OP.is_equal))
                V.op(lambda e, ti=ti: e.tensor_mul(
                    s_oh[:, ti * 8:(ti + 1) * 8], s_col[:, 0:8], s_idf[:, 0:8]))
                V.op(lambda e, ti=ti: e.scalar_tensor_tensor(
                    s_col[:, 0:8], s_oh[:, ti * 8:(ti + 1) * 8], -1e30,
                    s_rt[:, 0:8], OP.mult, OP.add))
                V.op(lambda e: e.reduce_max(s_rt[:, 10:11], s_col[:, 0:8],
                                            axis=AX.X))
                V.op(lambda e: e.tensor_scalar(
                    s_col[:, 0:8], s_col[:, 0:8], s_rt[:, 10:11], None,
                    OP.is_equal))
                V.op(lambda e: e.tensor_tensor_scan(
                    s_idf[:, 0:8], s_col[:, 0:8], s_col[:, 0:8], 0.0,
                    OP.add, OP.bypass))
                V.op(lambda e: e.tensor_scalar(
                    s_idf[:, 0:8], s_idf[:, 0:8], 1.0, None, OP.is_equal))
                tf2 = V.op(lambda e, ti=ti: e.tensor_mul(
                    s_oh2[:, ti * 8:(ti + 1) * 8], s_col[:, 0:8],
                    s_idf[:, 0:8]))
                te = S.op(lambda e: e.activation(
                    s_rt[:, 12:13], s_rt[:, 8:9], AF.Exp, scale=s_neg1[:, 0:1],
                    bias=s_rt[:, 10:11]), deps=[tf2])
                V.op(lambda e: e.tensor_scalar_add(
                    s_rt[:, 13:14], s_rt[:, 12:13], 1.0), deps=[te])
                V.op(lambda e: e.reciprocal(s_rt[:, 14:15], s_rt[:, 13:14]))
                tp2 = V.op(lambda e: e.tensor_mul(
                    s_rt[:, 15:16], s_rt[:, 12:13], s_rt[:, 14:15]))
                V.op(lambda e, ti=ti: e.tensor_copy(
                    s_pair[:, 4 * ti:4 * ti + 1], s_tokid[:, ti:ti + 1]))
                V.op(lambda e, ti=ti: e.tensor_copy(
                    s_pair[:, 4 * ti + 1:4 * ti + 2], s_rt[:, 14:15]))
                V.op(lambda e, ti=ti: e.tensor_copy(
                    s_pair[:, 4 * ti + 2:4 * ti + 3], s_tokid[:, ti:ti + 1]))
                V.op(lambda e, ti=ti: e.tensor_copy(
                    s_pair[:, 4 * ti + 3:4 * ti + 4], s_rt[:, 15:16]))
                V.op(lambda e, ti=ti: e.scalar_tensor_tensor(
                    s_col[:, 0:8], s_oh[:, ti * 8:(ti + 1) * 8], 1.0,
                    s_iota8[:], OP.mult, OP.mult,
                    accum_out=s_idf[:, 16 + ti:16 + ti + 1]))
                V.op(lambda e, ti=ti: e.scalar_tensor_tensor(
                    s_col[:, 0:8], s_oh2[:, ti * 8:(ti + 1) * 8], 1.0,
                    s_iota8[:], OP.mult, OP.mult,
                    accum_out=s_idf[:, 32 + ti:32 + ti + 1]))

            def emit_scan_k(cs, k):
                ckv, csg, crb = SETS[cs % 2]
                seed_ap = s_seed if cs == 0 else s_lastc
                ts = V.op(lambda e, k=k, seed_ap=seed_ap, ckv=ckv:
                          e.tensor_tensor_scan(
                              s_acc[k][:],
                              s_dec[:, k:k + 1].broadcast_to((128, CH)),
                              ckv[k][:],
                              seed_ap[:, k:k + 1], OP.mult, OP.add),
                          deps=[pre[(cs, k, "kv")], guard_T.get(cs - 1),
                                guard_S.get(cs - 1),
                                t_seed if cs == 0 else None])
                tl = V.op(lambda e, k=k: e.tensor_copy(
                    s_lastc[:, k:k + 1], s_acc[k][:, CH - 1:CH]), deps=[ts])
                t_wkv = V.op(lambda e, k=k, ckv=ckv: e.scalar_tensor_tensor(
                    s_wk[k][:], ckv[k][:], s_u[:, k:k + 1], s_acc[k][:],
                    OP.mult, OP.add),
                    deps=[tl, t_rw_all.get((cs - 1, k))])
                t_rw_all[(cs, k)] = G.op(
                    lambda e, k=k, crb=crb: e.tensor_mul(
                        s_acc[k][:], s_wk[k][:], crb[k][:]),
                    deps=[t_wkv, pre[(cs, k, "rb")]])

            for k in range(KC):
                emit_scan_k(0, k)
            xq_free = {0: None, 1: None}
            for c in range(NCH):
                ckv, csg, crb = SETS[c % 2]
                if c + 1 < NCH:
                    for k in range(KC):
                        aux = "qds0" if (c + 1) % 2 == 0 else "qds1"
                        nkv, nsg, nrb = SETS[(c + 1) % 2]
                        dma(Q, nkv[k][:], kv_sp[k, :, (c + 1) * CH:(c + 2) * CH],
                            deps=[prev_chunk_V, f_b1], aux=aux)
                        dma(Q, nsg[k][:], sg_sp[k, :, (c + 1) * CH:(c + 2) * CH],
                            deps=[prev_chunk_V, f_b1], aux=aux)
                        t0 = dma(Q, nrb[k][:],
                                 rb_sp[k, :, (c + 1) * CH:(c + 2) * CH],
                                 deps=[prev_chunk_V, f_b1], aux=aux)
                    for k in range(KC):
                        for w in ("kv", "sg", "rb"):
                            pre[(c + 1, k, w)] = t0
                for k in range(KC):
                    T.op(lambda e, k=k: e.matmul(
                        pb[0][0:16, 0:CH], s_indA[:, k * H:(k + 1) * H],
                        s_acc[k][:], start=(k == 0), stop=(k == KC - 1)),
                        deps=[t_rw_all[(c, k)]] +
                             ([bank_free[0]] if k == 0 else []))
                sq_mm = {}
                for k in range(KC):
                    scr = s_sq if k % 2 == 0 else s_l2
                    tsq = S.op(lambda e, k=k, scr=scr: e.activation(
                        scr[:], s_acc[k][:], AF.Square),
                        deps=[t_rw_all[(c, k)], sq_mm.get(k - 2)])
                    T.op(lambda e, k=k, scr=scr: e.matmul(
                        pb[1][0:16, 0:CH], s_indA[:, k * H:(k + 1) * H],
                        scr[:], start=(k == 0), stop=(k == KC - 1)),
                        deps=[tsq] + ([bank_free[1]] if k == 0 else []))
                    sq_mm[k] = T.tok()
                guard_T[c] = T.tok()
                guard_S[c] = S.tok()
                stats_done = T.tok()
                tm = V.op(lambda e: e.tensor_copy(s_hl[:, 0:CH], pb[0][0:16, 0:CH]),
                          deps=[stats_done])
                bank_free[0] = tm
                tv = V.op(lambda e: e.tensor_mul(
                    s_hl[:, CH:2 * CH], s_hl[:, 0:CH], s_hl[:, 0:CH]), deps=[tm])
                tv2 = V.op(lambda e: e.scalar_tensor_tensor(
                    s_hl[:, CH:2 * CH], s_hl[:, CH:2 * CH], -1.0,
                    pb[1][0:16, 0:CH], OP.mult, OP.add), deps=[tv])
                bank_free[1] = tv2
                tr1 = S.op(lambda e: e.activation(
                    s_hl[:, CH:2 * CH], s_hl[:, CH:2 * CH], AF.Ln,
                    bias=s_eps[0:16, 0:1]), deps=[tv2])
                tr2 = S.op(lambda e: e.activation(
                    s_hl[:, CH:2 * CH], s_hl[:, CH:2 * CH], AF.Exp,
                    scale=s_mhalf[0:16, 0:1]), deps=[tr1])
                t_opf = {}
                for k in range(KC):
                    bm = (2, 0)[k % 2]
                    brr = (3, 1)[k % 2]
                    T.op(lambda e, k=k, bm=bm: e.matmul(
                        pb[bm][:, 0:CH], s_indB[:, k * 128:(k + 1) * 128],
                        s_hl[:, 0:CH], start=True, stop=True),
                        deps=[tm, bank_free[bm]])
                    tmb = T.tok()
                    T.op(lambda e, k=k, brr=brr: e.matmul(
                        pb[brr][:, 0:CH], s_indB[:, k * 128:(k + 1) * 128],
                        s_hl[:, CH:2 * CH], start=True, stop=True),
                        deps=[tr2, bank_free[brr]])
                    trb = T.tok()
                    t1 = V.op(lambda e, k=k, bm=bm: e.scalar_tensor_tensor(
                        s_l1[:], s_acc[k][:], 1.0, pb[bm][:, 0:CH],
                        OP.mult, OP.subtract),
                        deps=[tmb, t_rw_all[(c, k)]])
                    bank_free[bm] = t1
                    t2 = V.op(lambda e, brr=brr: e.scalar_tensor_tensor(
                        s_l2[:], s_l1[:], 1.0, pb[brr][:, 0:CH],
                        OP.mult, OP.mult), deps=[trb])
                    bank_free[brr] = t2
                    t_opf[k] = V.op(lambda e, k=k, csg=csg: e.tensor_mul(
                        s_opf[k][:], s_l2[:], csg[k][:]),
                        deps=[t2, pre[(c, k, "sg")], prev_chunk_T])
                    S.op(lambda e, k=k: e.activation(
                        s_opb[k][:], s_opf[k][:], AF.Identity),
                        deps=[t_opf[k]])
                opb_done = S.tok()
                for k in range(KC):
                    T.op(lambda e, k=k: e.matmul(
                        pb[6][0:NEXP, 0:CH], s_M32[k][:], s_opf[k][:],
                        start=(k == 0), stop=(k == KC - 1)),
                        deps=[t_opf[k]] + ([bank_free[6], t_m32[-1]]
                                           if k == 0 else []))
                tlg = V.op(lambda e, c=c: e.scalar_tensor_tensor(
                    s_lg[:, c * CH:(c + 1) * CH], pb[6][0:NEXP, 0:CH], 1.0,
                    s_R1[:, c * CH:(c + 1) * CH], OP.mult, OP.add),
                    deps=[T.tok()])
                bank_free[6] = tlg
                tlg2 = V.op(lambda e, c=c: e.tensor_scalar(
                    s_lg[:, c * CH:(c + 1) * CH], s_lg[:, c * CH:(c + 1) * CH],
                    s_bb[0:NEXP, 0:1], None, OP.add), deps=[tlg])
                for tt4 in range(4):
                    lg_done[c * 4 + tt4] = tlg2
                st_info = {}

                def scan_pair(tt):
                    if c + 1 < NCH:
                        emit_scan_k(c + 1, 2 * tt)
                        emit_scan_k(c + 1, 2 * tt + 1)

                def finish_tile(tt):
                    ti = c * 4 + tt
                    pq = tt % 2
                    a = 4 * pq
                    ts2 = st_info[tt]
                    tsum = V.op(lambda e, a=a: e.tensor_add(
                        s_tcol[:, a:a + 1], s_tcol[:, a:a + 1],
                        s_tcol[:, a + 1:a + 2]), deps=[ts2])
                    tmn = V.op(lambda e, ti=ti, a=a: e.tensor_scalar_mul(
                        s_mn[:, ti:ti + 1], s_tcol[:, a:a + 1], -1.0 / D),
                        deps=[tsum])
                    q1 = S.op(lambda e, ti=ti, pq=pq, a=a: e.activation(
                        s_sq[:, 0:CH], s_xr2[pq][:, 0:512], AF.Square,
                        bias=s_mn[:, ti:ti + 1],
                        accum_out=s_tcol[:, a + 1:a + 2]), deps=[tmn])
                    q2 = S.op(lambda e, ti=ti, pq=pq, a=a: e.activation(
                        s_sq[:, 0:CH], s_xr2[pq][:, 512:1024], AF.Square,
                        bias=s_mn[:, ti:ti + 1],
                        accum_out=s_tcol[:, a + 2:a + 3]), deps=[q1])
                    scan_pair(tt)
                    tva = V.op(lambda e, a=a: e.tensor_add(
                        s_tcol[:, a + 1:a + 2], s_tcol[:, a + 1:a + 2],
                        s_tcol[:, a + 2:a + 3]), deps=[q2])
                    xq_free[pq] = tva
                    tvb = V.op(lambda e, a=a: e.tensor_scalar_mul(
                        s_tcol[:, a + 1:a + 2], s_tcol[:, a + 1:a + 2],
                        1.0 / D), deps=[tva])
                    r1 = S.op(lambda e, a=a: e.activation(
                        s_tcol[:, a + 3:a + 4], s_tcol[:, a + 1:a + 2], AF.Ln,
                        bias=s_eps[:, 0:1]), deps=[tvb])
                    r2 = S.op(lambda e, ti=ti, a=a: e.activation(
                        s_rstd[:, ti:ti + 1], s_tcol[:, a + 3:a + 4], AF.Exp,
                        scale=s_mhalf[:, 0:1]), deps=[r1])
                    xp = "qxp0" if ti % 2 == 0 else "qxp1"
                    xfp = "qxf0" if ti % 2 == 0 else "qxf1"
                    tx1o = V.op(lambda e, ti=ti, pq=pq: e.tensor_scalar(
                        s_x1[:], s_xr2[pq][:], s_mn[:, ti:ti + 1],
                        s_rstd[:, ti:ti + 1], OP.add, OP.mult),
                        deps=[r2, xp_tok.get(ti - 1), xfe_tok.get(ti - 1)])
                    tx1b = S.op(lambda e: e.activation(
                        s_x1b[:], s_x1[:], AF.Identity), deps=[tx1o])
                    x1sp_tok[ti] = dma(Q, x1_sp[ti * 128:(ti + 1) * 128, :],
                                       s_x1[:], deps=[tx1o], aux=xp)
                    xp_tok[ti] = x1sp_tok[ti]
                    xfe_tok[ti] = dma(Q, xf_in[ti * 128:(ti + 1) * 128, :],
                                      s_x1b[:],
                                      deps=[tx1b, xfe_tok.get(ti - 2)],
                                      aux=xfp)
                    if DBG:
                        xp_tok[ti] = dma(Q, dx1d[ti * 128:(ti + 1) * 128, :],
                                         s_x1[:], deps=[tx1o], aux=xp)

                for tt in range(4):
                    ti = c * 4 + tt
                    pq = tt % 2
                    t_xt = dma(S, s_xt[pq][:],
                               dxtok[ti * 128:(ti + 1) * 128, :],
                               deps=[xt_consumed.get(ti - 2)],
                               aux="sxt0" if ti % 2 == 0 else "sxt1")
                    for half in range(2):
                        bank = 4 + half
                        for k in range(KC):
                            T.op(lambda e, k=k, tt=tt, half=half, bank=bank:
                                 e.matmul(
                                     pb[bank][:, 0:512],
                                     s_opb[k][:, tt * 128:(tt + 1) * 128],
                                     s_woT[k][:, half * 512:(half + 1) * 512],
                                     start=(k == 0), stop=False),
                                 deps=[opb_done, wotl[k]] +
                                      ([bank_free[bank]] if k == 0 else []))
                        T.op(lambda e, half=half, bank=bank: e.matmul(
                            pb[bank][:, 0:512], s_ones[0:1, 0:128],
                            s_wobr[0:1, half * 512:(half + 1) * 512],
                            start=False, stop=True), deps=[t_wobr])
                    wo_done = T.tok()
                    tx1 = V.op(lambda e, pq=pq: e.tensor_add(
                        s_xr2[pq][:, 0:512], pb[4][:, 0:512],
                        s_xt[pq][:, 0:512]),
                        deps=[wo_done, t_xt, xq_free[pq]])
                    bank_free[4] = tx1
                    tx2 = V.op(lambda e, pq=pq: e.tensor_add(
                        s_xr2[pq][:, 512:1024], pb[5][:, 0:512],
                        s_xt[pq][:, 512:1024]), deps=[wo_done])
                    bank_free[5] = tx2
                    xt_consumed[ti] = tx2
                    ts1 = S.op(lambda e, pq=pq, a=4 * pq: e.activation(
                        s_sq[:, 0:CH], s_xr2[pq][:, 0:512], AF.Identity,
                        accum_out=s_tcol[:, a:a + 1]), deps=[tx1, tx2])
                    st_info[tt] = S.op(lambda e, pq=pq, a=4 * pq: e.activation(
                        s_sq[:, 0:CH], s_xr2[pq][:, 512:1024], AF.Identity,
                        accum_out=s_tcol[:, a + 1:a + 2]), deps=[ts1])
                    if tt >= 1:
                        finish_tile(tt - 1)
                        emit_router_tile(c * 4 + tt - 1)
                finish_tile(3)
                emit_router_tile(c * 4 + 3)
                prev_chunk_V = V.tok()
                prev_chunk_T = T.tok()
            f_xf = Q.flush()
            oh_all = V.tok()
            T.op(lambda e: e.matmul(pb[2][0:1, 0:128], s_ones[:, 0:1], s_oh[:],
                                    start=True, stop=True),
                 deps=[oh_all, bank_free[2]])
            tt1m = T.tok()
            T.op(lambda e: e.matmul(pb[3][0:1, 0:128], s_ones[:, 0:1], s_oh2[:],
                                    start=True, stop=True),
                 deps=[oh_all, bank_free[3]])
            tt2m = T.tok()
            tev1 = V.op(lambda e: e.tensor_copy(s_trow[:], pb[2][0:1, 0:128]),
                        deps=[tt1m])
            bank_free[2] = tev1
            tev2 = V.op(lambda e: e.tensor_copy(s_trow2[:], pb[3][0:1, 0:128]),
                        deps=[tt2m])
            bank_free[3] = tev2
            tcc = {}
            for cidx, (srow, sdst, tevx) in enumerate(
                    ((s_trow, s_tt16, tev1), (s_trow2, s_tt16b, tev2))):
                T.op(lambda e, srow=srow, cidx=cidx: e.transpose(
                    pb[cidx][:, 0:1], srow[0:1, 0:128], s_id8[0:1, 0:1]),
                    deps=[tevx, bank_free[cidx]])
                tcl1 = V.op(lambda e, cidx=cidx: e.tensor_copy(
                    s_col[:, 0:1], pb[cidx][:, 0:1]), deps=[T.tok()])
                bank_free[cidx] = tcl1
                tmul = V.op(lambda e: e.tensor_scalar_mul(
                    s_m8s[:], s_mask8[:], s_col[:, 0:1]), deps=[tcl1])
                T.op(lambda e: e.matmul(pb[6][0:16, 0:8], s_sel16[:],
                                        s_m8s[:], start=True, stop=True),
                     deps=[tmul, bank_free[6]])
                tcc[cidx] = V.op(lambda e, sdst=sdst: e.tensor_copy(
                    sdst[:], pb[6][0:16, 0:8]), deps=[T.tok()])
                bank_free[6] = tcc[cidx]
            tl1, tl2 = tcc[0], tcc[1]
            T.op(lambda e: e.matmul(pb[6][0:8, 0:1], s_tt16[:],
                                    s_ones[0:16, 0:1], start=True, stop=True),
                 deps=[tl1, bank_free[6]])
            T.op(lambda e: e.matmul(pb[6][0:8, 1:2], s_tt16b[:],
                                    s_ones[0:16, 0:1], start=False, stop=True),
                 deps=[tl2])
            tcv = V.op(lambda e: e.tensor_copy(s_cnt8[:], pb[6][0:8, 0:2]),
                       deps=[T.tok()])
            bank_free[6] = tcv
            dma(Q, cnt_in[0, 0:8], s_cnt8[0:8, 0:1], deps=[tcv])
            dma(Q, cnt_in[0, 8:16], s_cnt8[0:8, 1:2], deps=[tcv])
            f_ce = Q.flush()
            tok_cnt = coll("AllGather", OP.bypass, [cnt_in], [cnt_ag], ccnt,
                           deps=[f_ce])
            tok_xf = coll("AllGather", OP.bypass, [xf_in], [xf_ag], cxf,
                          deps=[f_xf])
            dma(Q, s_cnta[:], cnt_ag, deps=[tok_cnt])
            tcl = Q.flush()
            T.op(lambda e: e.matmul(pb[0][:, 0:128], s_lt128[:], s_oh[:],
                                    start=True, stop=True),
                 deps=[oh_all, bank_free[0]])
            rk1_grp = T.tok()
            T.op(lambda e: e.matmul(pb[1][:, 0:128], s_lt128[:], s_oh2[:],
                                    start=True, stop=True),
                 deps=[oh_all, bank_free[1]])
            rk2_grp = T.tok()
            T.op(lambda e: e.matmul(pb[2][0:16, 0:8], s_lt16[:], s_tt16[:],
                                    start=True, stop=True),
                 deps=[tl1, bank_free[2]])
            to1 = T.tok()
            T.op(lambda e: e.matmul(pb[3][0:16, 0:8], s_lt16[:], s_tt16b[:],
                                    start=True, stop=True),
                 deps=[tl2, bank_free[3]])
            to2 = T.tok()
            tof1 = V.op(lambda e: e.tensor_copy(s_tt16[:], pb[2][0:16, 0:8]),
                        deps=[to1])
            tof2 = V.op(lambda e: e.tensor_copy(s_tt16b[:], pb[3][0:16, 0:8]),
                        deps=[to2])
            tb3 = dma(Q, toff_b1, s_tt16[:], deps=[tof1])
            tb4 = dma(Q, toff_b2, s_tt16b[:], deps=[tof2])
            f_tb2 = Q.flush()
            dma(Q, s_trow[:], toff_b1.rearrange("t e -> (t e)"), deps=[f_tb2])
            dma(Q, s_trow2[:], toff_b2.rearrange("t e -> (t e)"), deps=[f_tb2])
            f_tlc = Q.flush()
            tl3 = tl4 = f_tlc
            T.op(lambda e: e.matmul(pb[0][:, 0:128], s_ones[0:1, :], s_trow[:],
                                    start=False, stop=True),
                 deps=[tl3, rk1_grp])
            rk1_done = T.tok()
            T.op(lambda e: e.matmul(pb[1][:, 0:128], s_ones[0:1, :], s_trow2[:],
                                    start=False, stop=True),
                 deps=[tl4, rk2_grp])
            rk2_done = T.tok()
            trke1 = V.op(lambda e: e.tensor_copy(s_rk1[:], pb[0][:, 0:128]),
                         deps=[rk1_done])
            bank_free[0] = trke1
            trke2 = V.op(lambda e: e.tensor_copy(s_rk2[:], pb[1][:, 0:128]),
                         deps=[rk2_done])
            bank_free[1] = trke2
            T.op(lambda e: e.matmul(pb[7][0:1, 0:16], s_selp[:, 0:1], s_cnta[:],
                                    start=True, stop=False),
                 deps=[tcl, bank7_free[0]])
            T.op(lambda e: e.matmul(pb[7][0:1, 8:16], s_selp[:, 1:2],
                                    s_cnta[:, 0:8], start=False, stop=True))
            toffr = V.op(lambda e: e.tensor_copy(s_offr[:], pb[7][0:1, 0:16]),
                         deps=[T.tok()])
            T.op(lambda e: e.matmul(pb[7][:, 0:16], s_ones[0:1, :], s_offr[:],
                                    start=True, stop=True), deps=[toffr])
            toffb = V.op(lambda e: e.tensor_copy(s_off[:], pb[7][:, 0:16]),
                         deps=[T.tok()])
            bank7_free[0] = toffb
            bank_free[7] = toffb

            scat_toks = []
            for ti in range(NT):
                for kk in range(2):
                    ohs = s_oh if kk == 0 else s_oh2
                    rks = s_rk1 if kk == 0 else s_rk2
                    rkd = trke1 if kk == 0 else trke2
                    tr = V.op(lambda e, ti=ti, ohs=ohs, rks=rks:
                              e.scalar_tensor_tensor(
                                  s_col[:, 0:8], ohs[:, ti * 8:(ti + 1) * 8],
                                  1.0, rks[:, ti * 8:(ti + 1) * 8],
                                  OP.mult, OP.mult, accum_out=s_rt[:, 0:1]),
                              deps=[rkd, toffb])
                    V.op(lambda e, ti=ti, kk=kk, ohs=ohs: e.scalar_tensor_tensor(
                        s_col[:, 0:8], ohs[:, ti * 8:(ti + 1) * 8], 1.0,
                        s_off[:, kk * 8:(kk + 1) * 8], OP.mult, OP.mult,
                        accum_out=s_rt[:, 1:2]), deps=[tr])
                    V.op(lambda e: e.tensor_add(s_rt[:, 2:3], s_rt[:, 0:1],
                                                s_rt[:, 1:2]))
                    V.op(lambda e: e.tensor_copy(s_ii[:, 0:1], s_rt[:, 2:3]))
                    V.op(lambda e, ti=ti, kk=kk: e.tensor_copy(
                        s_ii[:, 1:2],
                        s_idf[:, 16 * (kk + 1) + ti:16 * (kk + 1) + ti + 1]))
                    V.op(lambda e: e.tensor_scalar(
                        s_rt[:, 3:4], s_rt[:, 2:3], float(CAP), None, OP.is_lt))
                    V.op(lambda e: e.tensor_copy(s_ii[:, 2:3], s_rt[:, 3:4]))
                    V.op(lambda e: e.tensor_scalar(
                        s_ii[:, 3:4], s_ii[:, 0:1], 127, None, OP.bitwise_and))
                    V.op(lambda e: e.tensor_scalar(
                        s_ii[:, 3:4], s_ii[:, 3:4], 40, None, OP.mult))
                    V.op(lambda e: e.tensor_scalar(
                        s_ii[:, 4:5], s_ii[:, 0:1], 7, None,
                        OP.arith_shift_right))
                    V.op(lambda e: e.tensor_add(s_ii[:, 3:4], s_ii[:, 3:4],
                                                s_ii[:, 4:5]))
                    V.op(lambda e: e.tensor_scalar(
                        s_ii[:, 1:2], s_ii[:, 1:2], CAP, None, OP.mult))
                    V.op(lambda e: e.tensor_add(s_ii[:, 3:4], s_ii[:, 3:4],
                                                s_ii[:, 1:2]))
                    V.op(lambda e: e.tensor_scalar(
                        s_ii[:, 2:3], s_ii[:, 2:3], -1, None, OP.mult))
                    V.op(lambda e: e.tensor_scalar(
                        s_ii[:, 2:3], s_ii[:, 2:3], 1, None, OP.add))
                    V.op(lambda e: e.tensor_scalar(
                        s_ii[:, 2:3], s_ii[:, 2:3], IBIG, None, OP.mult))
                    o10 = V.op(lambda e, ti=ti, kk=kk: e.tensor_tensor(
                        s_ids[:, kk * NT + ti:kk * NT + ti + 1], s_ii[:, 3:4],
                        s_ii[:, 2:3], OP.add))
                    pcol = 4 * ti + 2 * kk
                    scat_toks.append(G.op(
                        lambda e, ti=ti, kk=kk, pcol=pcol: e.indirect_dma_start(
                            out=tab_in,
                            out_offset=IndirectOffsetOnAxis(
                                s_ids[:, kk * NT + ti:kk * NT + ti + 1], 0),
                            in_=s_pair[:, pcol:pcol + 2], in_offset=None,
                            bounds_check=_bc(e, NEXP * CAP - 1), oob_is_err=False),
                        deps=[o10, ZF], inc=16))
            tok_tab = coll("AllReduce", OP.add, [tab_in], [tab_ar], ctab,
                           deps=scat_toks)
            run_block(B2)
        P12.close()

        with ExitStack() as B3:
            s_cfc = [sbuf(B3, f"s_cfc{i}", [128, HDIM], BF16) for i in range(KC)]
            s_cpr = [sbuf(B3, f"s_cpr{i}", [128, D], BF16) for i in range(HKC)]
            s_tab = sbuf(B3, "s_tab", [128, 80], F32)
            s_gw = sbuf(B3, "s_gw", [128, 40], F32)
            s_gif = sbuf(B3, "s_gif", [128, 40], F32)
            s_gm = sbuf(B3, "s_gm", [128, 40], F32)
            s_gi = sbuf(B3, "s_gi", [128, 40], I32)
            s_gt = [sbuf(B3, f"s_gt{i}", [128, D], BF16) for i in range(2)]
            s_bch = [sbuf(B3, f"s_bch{i}", [128, SC], BF16) for i in range(KC)]
            s_h = [sbuf(B3, f"s_h{i}", [128, SC], BF16) for i in range(HKC)]
            s_eo = [sbuf(B3, f"s_eo{i}", [128, D], BF16) for i in range(8)]

            G.op(lambda e: e.indirect_dma_start(
                out=s_tab[:], out_offset=None, in_=tab_ar,
                in_offset=IndirectOffsetOnAxis(s_tbix[:, 0:1], 0),
                bounds_check=_bc(e, NEXP * CAP - 1), oob_is_err=False),
                deps=[tok_tab], inc=16)
            t_tb = G.flush()
            for k in range(KC):
                dma(G, s_cfc[k][:], dcfc[k * 128:(k + 1) * 128, :])
            f_cfc = G.flush()
            cfl = {k: f_cfc for k in range(KC)}
            for k2 in range(HKC):
                dma(G, s_cpr[k2][:], dcpr[k2 * 128:(k2 + 1) * 128, :])
            f_cpl = G.flush()
            cpl = {k2: f_cpl for k2 in range(HKC)}
            tgw = V.op(lambda e: e.tensor_copy(s_gw[:], s_tab[:, 1:80:2]),
                       deps=[t_tb])
            tgf = V.op(lambda e: e.tensor_copy(s_gif[:], s_tab[:, 0:80:2]),
                       deps=[t_tb])
            V.op(lambda e: e.tensor_scalar(
                s_gm[:], s_gif[:], 0.0, None, OP.is_equal), deps=[tgf])
            V.op(lambda e: e.tensor_scalar(
                s_gif[:], s_gif[:], -1.0, None, OP.add))
            V.op(lambda e: e.scalar_tensor_tensor(
                s_gif[:], s_gm[:], float(IBIG), s_gif[:], OP.mult, OP.add))
            tgi = V.op(lambda e: e.tensor_copy(s_gi[:], s_gif[:]))

            bank_free = {i: None for i in range(8)}
            gt_free = [None, None]
            eo_free = [None] * 8
            mm1_toks = {}
            mm2_prev = None
            rs_scat = {0: [], 1: []}
            tok_rs0 = None
            gts_all = {}

            def emit_gather(j):
                buf = j % 2
                tz = V.op(lambda e, buf=buf: e.memset(s_gt[buf][:], 0.0),
                          deps=[gt_free[buf]])
                tgt = G.op(lambda e, j=j, buf=buf: e.indirect_dma_start(
                    out=s_gt[buf][:], out_offset=None, in_=xf_ag,
                    in_offset=IndirectOffsetOnAxis(s_gi[:, j:j + 1], 0),
                    bounds_check=_bc(e, 8 * TOK - 1), oob_is_err=False),
                    deps=[tz, tgi, tok_xf], inc=0,
                    aux="ggt0" if j % 2 == 0 else "ggt1")
                tst = dma(Q, batch_d[j * 128:(j + 1) * 128, :],
                          s_gt[buf][:], deps=[tgt],
                          aux="qst0" if j % 2 == 0 else "qst1")
                gt_free[buf] = tst
                gts_all[j] = tst

            def emit_transposes(sc):
                for k in range(KC):
                    btok = S.op(
                        lambda e, sc=sc, k=k: e.dma_start_transpose(
                            s_bch[k][:],
                            batch_d[sc * SC:(sc + 1) * SC,
                                    k * 128:(k + 1) * 128]),
                        deps=[gts_all[sc * 4 + 2], gts_all[sc * 4 + 3],
                              mm1_toks.get(sc - 1)],
                        inc=0, aux="sbch0" if sc % 2 == 0 else "sbch1")
                bch_tok[sc] = btok

            bch_tok = {}
            for j in range(4 * NSC):
                emit_gather(j)
            emit_transposes(0)
            for sc in range(NSC):
                h_done = []
                for m2 in range(HKC):
                    bank = m2 % 6
                    for k in range(KC):
                        T.op(lambda e, m2=m2, k=k, bank=bank: e.matmul(
                            pb[bank][:, 0:SC],
                            s_cfc[k][:, m2 * 128:(m2 + 1) * 128],
                            s_bch[k][:],
                            start=(k == 0), stop=(k == KC - 1)),
                            deps=[bch_tok[sc], cfl[k]] +
                                 ([bank_free[bank]] if k == 0 else []))
                    hv = S.op(lambda e, m2=m2, bank=bank: e.activation(
                        s_h[m2][:], pb[bank][:, 0:SC], AF.Gelu),
                        deps=[T.tok(), mm2_prev])
                    bank_free[bank] = hv
                    h_done.append(hv)
                mm1_toks[sc] = T.tok()
                h_all = S.tok()
                for j4 in range(4):
                    j = sc * 4 + j4
                    ebuf = j % 8
                    for k2 in range(HKC):
                        for half in range(2):
                            T.op(lambda e, j4=j4, half=half, k2=k2:
                                 e.matmul(
                                     pb[6 + half][:, 0:512],
                                     s_h[k2][:, j4 * 128:(j4 + 1) * 128],
                                     s_cpr[k2][:, half * 512:(half + 1) * 512],
                                     start=(k2 == 0), stop=(k2 == HKC - 1)),
                                 deps=[h_all, cpl[k2]] +
                                      ([bank_free[6], bank_free[7]]
                                       if k2 == 0 and half == 0 else []))
                    mm2_tok = T.tok()
                    for half in range(2):
                        tev = V.op(lambda e, j=j, half=half, ebuf=ebuf:
                                   e.tensor_scalar_mul(
                                       s_eo[ebuf][:, half * 512:(half + 1) * 512],
                                       pb[6 + half][:, 0:512], s_gw[:, j:j + 1]),
                                   deps=[mm2_tok, eo_free[ebuf], tgw])
                        bank_free[6 + half] = tev
                    tsc = G.op(lambda e, j=j, ebuf=ebuf:
                               e.indirect_dma_start(
                                   out=rs_in,
                                   out_offset=IndirectOffsetOnAxis(
                                       s_gi[:, j:j + 1], 0),
                                   in_=s_eo[ebuf][:], in_offset=None,
                                   bounds_check=_bc(e, 8 * TOK - 1),
                                   oob_is_err=False),
                               deps=[V.tok(), ZF], inc=0,
                               aux=f"geo{j % 8}")
                    eo_free[ebuf] = tsc
                    rs_scat[0].append(tsc)
                mm2_prev = T.tok()
                if sc + 1 < NSC:
                    emit_transposes(sc + 1)
            tok_rs = coll("ReduceScatter", OP.add, [rs_in], [rs_out],
                          crs0, deps=rs_scat[0])
            run_block(B3)

        with ExitStack() as B4:
            s_f0 = [sbuf(B4, f"s_f0{i}", [128, D], BF16) for i in range(4)]
            s_x1c = [sbuf(B4, f"s_x1c{i}", [128, D], F32) for i in range(NT)]
            s_xr = [sbuf(B4, f"s_xr{i}", [128, D], F32) for i in range(4)]
            s_o = [sbuf(B4, f"s_o{i}", [128, D], F32) for i in range(2)]
            s_c4 = sbuf(B4, "s_c4", [128, 4 * NT], F32)
            s_sq4 = sbuf(B4, "s_sq4", [128, 512], F32)
            cons_tok = {}
            out_tok = {}
            stage = {}
            for ti in range(NT):
                lxt = dma(Q, s_x1c[ti][:], x1_sp[ti * 128:(ti + 1) * 128, :],
                          deps=[f_xf], aux="qst0")
            lx_all = Q.flush()

            def finB1(ti):
                a = 4 * ti
                q2 = stage[ti]
                tva = V.op(lambda e, a=a: e.tensor_add(
                    s_c4[:, a + 2:a + 3], s_c4[:, a + 2:a + 3],
                    s_c4[:, a + 3:a + 4]), deps=[q2])
                tvb = V.op(lambda e, a=a: e.tensor_scalar_mul(
                    s_c4[:, a + 2:a + 3], s_c4[:, a + 2:a + 3], 1.0 / D),
                    deps=[tva])
                r1 = S.op(lambda e, a=a: e.activation(
                    s_c4[:, a + 3:a + 4], s_c4[:, a + 2:a + 3], AF.Ln,
                    bias=s_eps[:, 0:1]), deps=[tvb])
                S.op(lambda e, a=a: e.activation(
                    s_c4[:, a:a + 1], s_c4[:, a + 3:a + 4], AF.Exp,
                    scale=s_mhalf[:, 0:1]), deps=[r1])

            def finB2(ti):
                a = 4 * ti
                rdone = S.tok()
                to = V.op(lambda e, ti=ti, a=a: e.tensor_scalar(
                    s_o[ti % 2][:], s_xr[ti % 4][:], s_c4[:, a + 1:a + 2],
                    s_c4[:, a:a + 1], OP.add, OP.mult),
                    deps=[rdone, out_tok.get(ti - 2)])
                out_tok[ti] = dma(Q, dout[ti * 128:(ti + 1) * 128, :],
                                  s_o[ti % 2][:], deps=[to],
                                  aux="qso0" if ti % 2 == 0 else "qso1")

            for ti in range(NT):
                a = 4 * ti
                fx = "qf0" if ti % 2 == 0 else "qf1"
                l0 = dma(S, s_f0[ti % 4][:], rs_out[ti * 128:(ti + 1) * 128, :],
                         deps=[tok_rs, lx_all, cons_tok.get(ti - 4)],
                         aux="sxt0" if ti % 2 == 0 else "sxt1")
                tb = V.op(lambda e, ti=ti: e.tensor_add(
                    s_xr[ti % 4][:], s_f0[ti % 4][:], s_x1c[ti][:]),
                    deps=[l0])
                cons_tok[ti] = tb
                tsum = V.op(lambda e, a=a, ti=ti: e.reduce_sum(
                    s_c4[:, a:a + 1], s_xr[ti % 4][:], axis=AX.X), deps=[tb])
                V.op(lambda e, a=a: e.tensor_scalar_mul(
                    s_c4[:, a + 1:a + 2], s_c4[:, a:a + 1], -1.0 / D),
                    deps=[tsum])
                tmn = V.tok()
                q1 = S.op(lambda e, ti=ti, a=a: e.activation(
                    s_sq4[:], s_xr[ti % 4][:, 0:512], AF.Square,
                    bias=s_c4[:, a + 1:a + 2],
                    accum_out=s_c4[:, a + 2:a + 3]), deps=[tmn])
                stage[ti] = S.op(lambda e, ti=ti, a=a: e.activation(
                    s_sq4[:], s_xr[ti % 4][:, 512:1024], AF.Square,
                    bias=s_c4[:, a + 1:a + 2],
                    accum_out=s_c4[:, a + 3:a + 4]), deps=[q1])
                if ti >= 1:
                    finB1(ti - 1)
                if ti >= 2:
                    finB2(ti - 2)
            finB1(NT - 1)
            finB2(NT - 2)
            finB2(NT - 1)
            Q.wait([out_tok[NT - 1], out_tok[NT - 2]])
            run_block(B4)

    return nc


def _host_inputs(inputs, core):
    b, j = core // 2, core % 2
    x = np.asarray(inputs["L0_attn_x"])
    xb = x[b]
    lo = j * TOK
    bidx = (lo - 1) % N
    xch = np.ascontiguousarray(
        np.concatenate([xb.T[:, bidx:bidx + 1], xb.T[:, lo:lo + TOK]], axis=1))
    xtok = np.ascontiguousarray(xb[lo:lo + TOK])

    def vec128(v):
        return np.ascontiguousarray(np.asarray(v, np.float32).reshape(KC, 128).T)

    selc = np.zeros((8, 1), np.float32)
    if j == 1:
        selc[core - 1, 0] = 1.0
    selp = np.zeros((8, 2), np.float32)
    selp[:core, 0] = 1.0
    selp[:, 1] = 1.0
    tokid = (np.float32(core * TOK + 1)
             + np.arange(128, dtype=np.float32)[:, None]
             + 128.0 * np.arange(NT, dtype=np.float32)[None, :])
    tabix = (core * CAP + 40 * np.arange(128, dtype=np.int32))[:, None]

    return dict(
        xch=xch, xtok=xtok,
        wgT=np.ascontiguousarray(np.asarray(inputs["L0_attn_Wg_w"]).T),
        wkT=np.ascontiguousarray(np.asarray(inputs["L0_attn_Wk_w"]).T),
        wvT=np.ascontiguousarray(np.asarray(inputs["L0_attn_Wv_w"]).T),
        wrT=np.ascontiguousarray(np.asarray(inputs["L0_attn_Wr_w"]).T),
        woT=np.ascontiguousarray(np.asarray(inputs["L0_attn_Wo_w"]).T),
        won=np.ascontiguousarray(np.asarray(inputs["L0_attn_Wo_w"])),
        bg=vec128(inputs["L0_attn_Wg_b"]), bk=vec128(inputs["L0_attn_Wk_b"]),
        bv=vec128(inputs["L0_attn_Wv_b"]), br=vec128(inputs["L0_attn_Wr_b"]),
        wob=np.ascontiguousarray(np.asarray(inputs["L0_attn_Wo_b"],
                                            np.float32)).reshape(1, D),
        mg=vec128(inputs["L0_attn_mg"]), mk=vec128(inputs["L0_attn_mk"]),
        mv=vec128(inputs["L0_attn_mv"]), mr=vec128(inputs["L0_attn_mr"]),
        wdec=vec128(np.asarray(inputs["L0_attn_w"]).reshape(-1)),
        uu=vec128(np.asarray(inputs["L0_attn_u"]).reshape(-1)),
        wg=np.ascontiguousarray(np.asarray(inputs["L0_ffn_w_g"])),
        cfc=np.ascontiguousarray(np.asarray(inputs["L0_ffn_c_fc"])[core]),
        cpr=np.ascontiguousarray(np.asarray(inputs["L0_ffn_c_proj"])[core]),
        lt128=np.ascontiguousarray(np.tril(np.ones((128, 128), np.float32), -1)),
        lt16=np.ascontiguousarray(np.tril(np.ones((16, 16), np.float32), -1)),
        ones128=np.ones((128, 128), np.float32),
        id8=np.eye(8, dtype=np.float32),
        iota8=np.ascontiguousarray(np.tile(np.arange(8, dtype=np.float32),
                                           (128, 1))),
        iorev=np.ascontiguousarray(np.tile(
            np.arange(CH, dtype=np.float32) - (CH - 1), (128, 1))),
        indA=np.ascontiguousarray(
            np.repeat(np.eye(H, dtype=np.float32), 64, axis=0) / 64.0),
        indB=np.ascontiguousarray(
            np.repeat(np.eye(H, dtype=np.float32), 64, axis=1)),
        selc=selc, selp=selp, tokid=np.ascontiguousarray(tokid),
        wobc=vec128(inputs["L0_attn_Wo_b"]),
        tabix=np.ascontiguousarray(tabix.astype(np.int32)),
        mask8=np.tile(np.eye(8, dtype=np.float32), (16, 1)),
        sel16=np.repeat(np.eye(16, dtype=np.float32), 8, axis=0),
    )


_NC_CACHE = {}


def kernel(**inputs):
    if "nc" not in _NC_CACHE:
        _NC_CACHE["nc"] = build_nc()
    nc = _NC_CACHE["nc"]
    in_maps = [_host_inputs(inputs, c) for c in range(8)]
    res = run_bass_kernel_spmd(nc, in_maps, core_ids=list(range(8)))
    out = np.stack([np.asarray(res.results[c]["out"]) for c in range(8)], axis=0)
    return np.ascontiguousarray(out.reshape(B, N, D).astype(np.float32))
